# version 13
# baseline (speedup 1.0000x reference)
import numpy as np
import ml_dtypes
import concourse.bass as bass
import concourse.mybir as mybir
from concourse.bass_utils import run_bass_kernel_spmd

F32 = mybir.dt.float32
BF16 = mybir.dt.bfloat16
AX = mybir.AxisListType.X
AFT = mybir.ActivationFunctionType
ALU = mybir.AluOpType

D = 1024
ND = 8
NH = 16
DH = 64
NP = 8
NE = 8
FF = 4096
NFC = 32
EPS = 1e-5
ISQ = 0.125


class Sem:
    __slots__ = ("h", "n")

    def __init__(self, h):
        self.h = h
        self.n = 0


def chunks(lo, hi, step):
    out = []
    while lo < hi:
        out.append((lo, min(lo + step, hi)))
        lo += step
    return out


class Builder:
    def __init__(self):
        self.streams = {k: [] for k in ("t", "v", "s", "g", "y")}
        self.meta = []
        self.selfsem = {}
        self.stopped = False

    def em(self, eng, op, waits=(), inc=None):
        if self.stopped:
            return
        wl = [(s, v) for (s, v) in waits if v > 0]
        ss = self.selfsem.get(eng)
        if ss is not None:
            if ss.n > 0:
                wl.append((ss, ss.n))
            inc = (ss, 1) if inc is None else inc
            assert inc[0] is ss, "self-serialized engine must inc its own sem"
        incv = None
        if inc is not None:
            sem, n = inc
            sem.n += n
            incv = (sem, n)
        self.meta.append((eng, wl, incv))

        def f(e, op=op, wl=tuple(wl), incv=incv):
            for (s, v) in wl:
                e.wait_ge(s.h, v)
            ins = op(e)
            if incv is not None:
                ins.then_inc(incv[0].h, incv[1])
        self.streams[eng].append(f)

    def raw(self, eng, fn, waits=()):
        if self.stopped:
            return
        wl = [(s, v) for (s, v) in waits if v > 0]
        self.meta.append((eng, wl, None))

        def f(e, fn=fn, wl=tuple(wl)):
            for (s, v) in wl:
                e.wait_ge(s.h, v)
            fn(e)
        self.streams[eng].append(f)

    def check_deadlock(self):
        by_eng = {k: [] for k in self.streams}
        for (eng, waits, incv) in self.meta:
            by_eng[eng].append((waits, incv))
        ptr = {k: 0 for k in by_eng}
        counts = {}
        progressed = True
        while progressed:
            progressed = False
            for k in by_eng:
                while ptr[k] < len(by_eng[k]):
                    waits, incv = by_eng[k][ptr[k]]
                    if all(counts.get(id(s), 0) >= v for (s, v) in waits):
                        if incv is not None:
                            s, n = incv
                            counts[id(s)] = counts.get(id(s), 0) + n
                        ptr[k] += 1
                        progressed = True
                    else:
                        break
        stuck = {k: ptr[k] for k in by_eng if ptr[k] < len(by_eng[k])}
        if stuck:
            det = []
            for k in stuck:
                waits, _ = by_eng[k][ptr[k]]
                det.append(f"{k}@{ptr[k]}/{len(by_eng[k])} needs "
                           f"{[(v, counts.get(id(s), 0)) for s, v in waits]}")
            raise RuntimeError("emitted program deadlocks: " + "; ".join(det))


class _StopEmit(Exception):
    pass


def build_nc(T=2048, dbg=False, upto=99):
    nc = bass.Bass("TRN2", target_bir_lowering=False)
    NQT = T // 128
    TH = T // 2
    B = Builder()
    em = B.em

    def din(name, shape):
        return nc.dram_tensor(name, list(shape), F32, kind="ExternalInput")

    xT = din("xT", (ND, 128, T))
    c_col = din("c_col", (128, ND))
    wmodT = din("wmodT", (128, 48 * ND * 128))
    gamma_c = din("gamma_c", (128, ND))
    beta_c = din("beta_c", (128, ND))
    wqkT = din("wqkT", (NP, 2, 128, ND * 128))
    wvT = din("wvT", (ND, 128, D))
    woT = din("woT", (ND, 128, ND * 128))
    wgT = din("wgT", (128, ND * NE))
    alpha_tok = din("alpha_tok", (128, NQT * NE))
    lnw_c = din("lnw_c", (128, NE * ND))
    lnb_c = din("lnb_c", (128, NE * ND))
    fc1T = din("fc1T", (NE, NFC // 2, 128, 2 * ND * 128))
    fc1b_c = din("fc1b_c", (128, NE * NFC))
    fc2T = din("fc2T", (NE, ND, 128, NFC * 128))
    fc2b_cD = din("fc2b_cD", (128, NE * ND))
    identF = din("identF", (128, 128))
    maskC_F = din("maskC_F", (2, 128, 896))
    idxmF = din("idxmF", (128, (T // 128) * NE))
    outT = nc.dram_tensor("outT", [ND, 128, T], F32, kind="ExternalOutput")
    dbg_t = {}
    if dbg:
        for nm, sh in (("d_mod", (128, 48)), ("d_xn", (128, ND * T)),
                       ("d_ctx", (128, ND * T)), ("d_x2", (128, ND * T)),
                       ("d_xn2", (128, ND * T)), ("d_comb", (128, (T // 128) * NE)),
                       ("d_lg", (NE, T)), ("d_hdn", (128, ND * (T // 2))),
                       ("d_A", (128, T // 2)), ("d_acc", (128, ND * (T // 2)))):
            dbg_t[nm] = nc.dram_tensor(nm, list(sh), F32, kind="ExternalOutput")

    from contextlib import ExitStack
    with ExitStack() as es:
        sbuf = lambda n, sh, dt: es.enter_context(nc.sbuf_tensor(n, list(sh), dt))
        psum = lambda n, sh, dt: es.enter_context(nc.psum_tensor(n, list(sh), dt))

        x_sb = sbuf("x_sb", (128, ND * T), BF16)
        hx = sbuf("hx", (128, ND * T), BF16)
        mu_b = sbuf("mu_b", (128, T), BF16)
        inv_b = sbuf("inv_b", (128, T), BF16)
        rows_bf = sbuf("rows_bf", (1, T), BF16)
        m_all = sbuf("m_all", (128, 2 * NQT + 2), BF16)
        mod_sb = sbuf("mod_sb", (128, 48), F32)
        ident = sbuf("ident", (128, 128), BF16)
        identn = sbuf("identn", (128, 128), BF16)
        maskC = sbuf("maskC", (128, 2 * 896), BF16)
        onescol = sbuf("onescol", (128, 1), BF16)
        onesrow = sbuf("onesrow", (1, 128), BF16)
        c1024 = sbuf("c1024", (1, 128), BF16)
        eps_col = sbuf("eps_col", (128, 1), F32)
        tiny_col = sbuf("tiny_col", (128, 1), F32)
        ccol_sb = sbuf("ccol_sb", (128, ND), F32)
        siluc_sb = sbuf("siluc_sb", (128, ND), BF16)
        gam_sb = sbuf("gam_sb", (128, ND), F32)
        bet_sb = sbuf("bet_sb", (128, ND), F32)
        lnw_sb = sbuf("lnw_sb", (128, NE * ND), F32)
        lnb_sb = sbuf("lnb_sb", (128, NE * ND), F32)
        fc1b_sb = sbuf("fc1b_sb", (128, NE * NFC), BF16)
        wg_sb = sbuf("wg_sb", (128, ND * NE), BF16)
        alpha_sb = sbuf("alpha_sb", (128, NQT * NE), BF16)
        fc2bc_sb = sbuf("fc2bc_sb", (128, NE * ND), F32)
        gtok_sb = sbuf("gtok_sb", (128, NQT * NE), BF16)
        g1_sb = sbuf("g1_sb", (128, NQT * NE), BF16)
        g2_sb = sbuf("g2_sb", (128, NQT * NE), BF16)
        g3_sb = sbuf("g3_sb", (128, 3 * NQT), F32)
        idxm_sb = sbuf("idxm_sb", (128, NQT * NE), BF16)

        o_vaug = 0
        o_ctx = o_vaug + NQT * 1536
        o_QmA = o_ctx + ND * T
        o_QmB = o_QmA + T
        o_KaA = o_QmB + T
        o_KaB = o_KaA + T
        o_QmC = o_KaB + T
        o_KaC = o_QmC + T
        AR1 = o_KaC + T
        o_A = 0
        o_hdn = o_A + NFC * TH
        o_fc2 = o_hdn + ND * TH
        o_fc1 = o_fc2 + 2 * 2048
        o_acc = o_fc1 + 2 * 2048
        if T >= 2048:
            ARN = max(o_acc + ND * TH, AR1, 48 * ND * 128 + 0)
            o_wv = o_QmA
            o_wo = o_acc
            o_wqk = AR1
            assert o_wv + ND * 1024 <= ARN and o_wqk + 2 * 1024 <= ARN
        else:
            base2 = max(AR1, o_acc + ND * TH)
            o_wqk = base2
            o_wo = base2 + 2 * 1024
            o_wv = base2 + 4 * 1024
            ARN = max(o_wv + ND * 1024, 48 * ND * 128)
        arena = sbuf("arena", (128, ARN), BF16)
        af = sbuf("af", (128, max(T, 3 * 512)), F32)
        combb_sb = sbuf("combb_sb", (128, max(TH, 512)), BF16)
        o_out = 0
        row_f = af[0:1, 0:T]
        o_wmod = 0

        lg_sb = arena[0:NE, o_ctx:o_ctx + T]
        tmp1 = arena[:, o_ctx + T:o_ctx + 2 * T]
        row_s1 = arena[0:1, o_ctx + 2 * T:o_ctx + 3 * T]
        row_s2 = arena[0:1, o_ctx + 3 * T:o_ctx + 4 * T]
        vaug = lambda kc: arena[:, o_vaug + kc * 1536: o_vaug + (kc + 1) * 1536]
        ctx = lambda p: arena[:, o_ctx + p * T: o_ctx + (p + 1) * T]
        QmA = arena[0:65, o_QmA:o_QmA + T]
        QmB = arena[0:65, o_QmB:o_QmB + T]
        QmC = arena[0:65, o_QmC:o_QmC + T]
        KaA = arena[0:65, o_KaA:o_KaA + T]
        KaB = arena[0:65, o_KaB:o_KaB + T]
        KaC = arena[0:65, o_KaC:o_KaC + T]

        def qm_of(h):
            p, hl = divmod(h, 2)
            return QmA if hl == 0 else (QmB if p % 2 == 0 else QmC)

        def ka_of(h):
            p, hl = divmod(h, 2)
            return KaA if hl == 0 else (KaB if p % 2 == 0 else KaC)
        pt = [mu_b[:, 0:min(T, 1024)], (mu_b[:, 1024:2048] if T >= 2048 else inv_b[:, 0:min(T, 1024)])]
        wqk_buf = lambda i: arena[:, o_wqk + (i % 2) * 1024: o_wqk + (i % 2) * 1024 + 1024]
        wo_buf = lambda i: arena[:, o_wo + (i % 2) * 1024: o_wo + (i % 2) * 1024 + 1024]
        A_ap = lambda fc: arena[:, o_A + fc * TH: o_A + (fc + 1) * TH]
        hdn_ap = lambda dc: arena[:, o_hdn + dc * TH: o_hdn + (dc + 1) * TH]
        fc1buf2 = lambda i: arena[:, o_fc1 + (i % 2) * 2048: o_fc1 + (i % 2) * 2048 + 2048]
        fc2buf = lambda i: arena[:, o_fc2 + (i % 2) * 2048: o_fc2 + (i % 2) * 2048 + 2048]
        acc_ap = lambda hcx: arena[:, o_acc + hcx * TH: o_acc + (hcx + 1) * TH]
        acc_ap_l = acc_ap
        combb = lambda i: combb_sb[:, 0:TH]
        out_st = lambda i: af[:, o_out + (i % 2) * 512: o_out + (i % 2) * 512 + 512]

        psBig = psum("psBig", (128, 2048), F32)
        psB2 = psum("psB2", (128, 1024), F32)
        psC2 = psum("psC2", (128, 1024), F32)

        mk_sem = lambda n: Sem(es.enter_context(nc.semaphore(n)))
        sT, sV, sS, sM = mk_sem("sT"), mk_sem("sV"), mk_sem("sS"), mk_sem("sM")
        sX, sC, sW0, sWV = mk_sem("sX"), mk_sem("sC"), mk_sem("sW0"), mk_sem("sWV")
        sQK = [mk_sem("sQK0"), mk_sem("sQK1")]
        sWO = [mk_sem("sWO0"), mk_sem("sWO1")]
        sF1 = [mk_sem("sF10"), mk_sem("sF11")]
        sF2 = [mk_sem("sF20"), mk_sem("sF21")]
        sSM = mk_sem("sSM")
        sO = [mk_sem("sO0"), mk_sem("sO1")]
        B.selfsem = {"v": sV, "s": sS}
        sDB = mk_sem("sDB")

        def dbgj():
            return [(sV, sV.n)] if dbg else []

        def gate(k):
            if upto < k:
                B.stopped = True

        def tap(nm, src, waits=()):
            if not dbg or (dbg is not True and nm not in dbg):
                return
            total = dbg_t[nm].shape[-1]
            csz = min(total, 512)
            buf = es.enter_context(nc.sbuf_tensor("b_" + nm, [dbg_t[nm].shape[0], csz], F32))
            for j, (a, b) in enumerate(chunks(0, total, csz)):
                em("v", lambda e, a=a, b=b, buf=buf, s_=src:
                   e.tensor_copy(buf[:, 0:b - a], s_[:, a:b]),
                   list(waits) + ([(sDB, sDB.n)] if j else []), (sV, 1))
                em("y", lambda e, a=a, b=b, buf=buf, nm=nm:
                   e.dma_start(dbg_t[nm].ap()[:, a:b], buf[:, 0:b - a]),
                   [(sV, sV.n)], (sDB, 16))

        FREE = {}

        def free_of(name):
            return list(FREE.get(name, ()))

        def set_free(name, *marks):
            FREE[name] = [(s, v) for (s, v) in marks]

        def add_free(name, sem, val):
            FREE.setdefault(name, []).append((sem, val))

        def gdma(sem, dst, src, waits=()):
            em("g", lambda e, d=dst, s=src: e.dma_start(d, s), waits, (sem, 16))
            return sem.n

        def ydma(sem, dst, src, waits=()):
            em("y", lambda e, d=dst, s=src: e.dma_start(d, s), waits, (sem, 16))
            return sem.n

        def MM(out, lhsT, rhs, start, stop, waits=(), inc=None, tp=None):
            def op(e, o=out, l=lhsT, r=rhs, st=start, sp=stop, tp=tp):
                kw = {"skip_group_check": True}
                if tp is not None:
                    kw["tile_position"] = tp
                return e.matmul(o, l, r, start=st, stop=sp, **kw)
            em("t", op, waits, inc)

        for ap_, val in ((onescol[:], 1.0), (onesrow[:], 1.0), (c1024[:], 1.0 / D),
                         (eps_col[:], float(EPS)), (tiny_col[:], 1e-30)):
            em("g", lambda e, a=ap_, v=val: e.memset(a, v), (), (sM, 1))
        mk_memset = sM.n

        for dst, src in ((ccol_sb[:], c_col.ap()), (gam_sb[:], gamma_c.ap()),
                         (bet_sb[:], beta_c.ap()), (lnw_sb[:], lnw_c.ap()),
                         (lnb_sb[:], lnb_c.ap()),
                         (fc2bc_sb[:], fc2b_cD.ap())):
            ydma(sSM, dst, src)
        mk_smalls = sSM.n
        gdma(sC, alpha_sb[:], alpha_tok.ap())
        gdma(sC, ident[:], identF.ap())
        gdma(sC, maskC[:, 0:896], maskC_F.ap()[0])
        gdma(sC, maskC[:, 896:2 * 896], maskC_F.ap()[1])
        gdma(sC, wg_sb[:], wgT.ap())
        gdma(sC, idxm_sb[:], idxmF.ap())
        gdma(sC, fc1b_sb[:], fc1b_c.ap())
        mk_consts_g = sC.n
        em("v", lambda e: e.tensor_scalar_mul(identn[:], ident[:], -1.0),
           [(sC, mk_consts_g)], (sV, 1))
        mk_x = []
        for dc in range(ND):
            mk_x.append(gdma(sX, x_sb[:, dc * T:(dc + 1) * T], xT.ap()[dc]))
        for i6 in range(8):
            w_ = 48 * ND * 128 // 8
            gdma(sW0, arena[:, o_wmod + i6 * w_:o_wmod + (i6 + 1) * w_],
                 wmodT.ap()[:, i6 * w_:(i6 + 1) * w_])
        mk_wmod = sW0.n
        mk_wv = []

        em("s", lambda e: e.activation(siluc_sb[:], ccol_sb[:], AFT.Silu),
           [(sSM, mk_smalls)], (sS, 1))
        mk_silu = sS.n

        for mc in range(48):
            for dc in range(ND):
                MM(psC2[:, mc:mc + 1],
                   arena[:, o_wmod + (mc * ND + dc) * 128:o_wmod + (mc * ND + dc) * 128 + 128],
                   siluc_sb[:, dc:dc + 1], start=(dc == 0), stop=(dc == ND - 1),
                   waits=([(sW0, mk_wmod), (sS, mk_silu)] if (mc == 0 and dc == 0) else ()),
                   inc=((sT, 1) if (mc == 47 and dc == ND - 1) else None))
        mk_modmm = sT.n
        for dc in range(ND):
            mk_wv.append(gdma(sWV, arena[:, o_wv + dc * 1024:o_wv + (dc + 1) * 1024],
                              wvT.ap()[dc], waits=[(sT, mk_modmm)] if dc == 0 else ()))
        em("v", lambda e: e.tensor_copy(mod_sb[:], psC2[:, 0:48]), [(sT, mk_modmm)], (sV, 1))
        mk_modev = sV.n
        set_free("C2a", (sV, mk_modev))
        set_free("C2b", (sV, mk_modev))
        em("v", lambda e: e.tensor_scalar_add(mod_sb[:, 8:16], mod_sb[:, 8:16], 1.0),
           [(sV, mk_modev)])
        em("v", lambda e: e.tensor_scalar_add(mod_sb[:, 32:40], mod_sb[:, 32:40], 1.0))
        for base in (16, 40):
            sl = mod_sb[:, base:base + 8]
            t_ = g1_sb[:, 0:8]
            s_ = g2_sb[:, 0:8]
            em("s", lambda e, sl=sl, s_=s_: e.activation(s_, sl, AFT.Sign, bias=tiny_col[:]),
               [(sV, sV.n), (sM, mk_memset)], (sS, 1))
            mk_sgn = sS.n
            em("v", lambda e, sl=sl, t_=t_, s_=s_: e.tensor_mul(t_, sl, s_), [(sS, mk_sgn)])
            em("v", lambda e, t_=t_: e.tensor_scalar_max(t_, t_, 0.25))
            em("v", lambda e, sl=sl, t_=t_, s_=s_: e.tensor_mul(sl, t_, s_), (), (sV, 1))
        mk_mod = sV.n
        tap("d_mod", mod_sb[:])
        gate(1)

        xc = lambda dc: x_sb[:, dc * T:(dc + 1) * T]
        hc = lambda dc: hx[:, dc * T:(dc + 1) * T]

        def drive(g):
            try:
                while True:
                    next(g)
            except StopIteration as st:
                return st.value

        def ln_stats(src_of, w_first, w_dc=None, cr=None, wj0=None):
            c0, c1 = cr if cr is not None else (0, T)
            cw = c1 - c0
            rnames = (["BigL", "BigH"] if (c0 == 0 and c1 > 1024) else
                      (["BigL"] if c0 == 0 else ["BigH"]))
            wj = (list(w_first) + [(sV, sV.n), (sS, sS.n)]) if wj0 is None \
                else list(w_first) + list(wj0)
            wd = (lambda dc: []) if w_dc is None else w_dc
            frs = sum((free_of(rn) for rn in rnames), [])
            for dc in range(ND):
                for (a, b) in chunks(c0, c1, 512):
                    w = list(wd(dc)) if a == c0 else []
                    if dc == 0 and a == c0:
                        w += wj + [(sM, mk_memset)] + frs
                    MM(psBig[0:1, a:b], onescol[:], src_of(dc)[:, a:b],
                       start=(dc == 0), stop=(dc == ND - 1), waits=w,
                       inc=((sT, 1) if (dc == ND - 1 and b == c1) else None))
                yield
            mk_s1 = sT.n
            em("v", lambda e: e.tensor_copy(row_s1[:, c0:c1], psBig[0:1, c0:c1]),
               [(sT, mk_s1)], (sV, 1))
            mk_s1e = sV.n
            mm_dc = {}
            for dc in range(ND):
                sq = mu_b if dc % 2 == 0 else inv_b
                sw = list(wd(dc)) + ([(sT, mm_dc[dc - 2])] if dc >= 2 else list(wj))
                em("s", lambda e, s_=sq, x_=src_of(dc):
                   e.activation(s_[:, c0:c1], x_[:, c0:c1], AFT.Square), sw, (sS, 1))
                mkq = sS.n
                for (a, b) in chunks(c0, c1, 512):
                    MM(psBig[0:1, a:b], onescol[:], sq[:, a:b],
                       start=(dc == 0), stop=(dc == ND - 1),
                       waits=([(sS, mkq)] + ([(sV, mk_s1e)] if dc == 0 else []))
                       if a == c0 else (),
                       inc=(sT, 1) if b == c1 else None)
                mm_dc[dc] = sT.n
                yield
            mk_s2 = sT.n
            em("v", lambda e: e.tensor_copy(row_s2[:, c0:c1], psBig[0:1, c0:c1]),
               [(sT, mk_s2)], (sV, 1))
            mk_s2e = sV.n
            for (a, b) in chunks(c0, c1, 512):
                MM(psBig[:, a:b], c1024[:], row_s1[0:1, a:b], True, True,
                   waits=[(sV, mk_s2e)] if a == c0 else (),
                   inc=(sT, 1) if b == c1 else None)
            mk_bc1 = sT.n
            em("v", lambda e: e.tensor_copy(mu_b[:, c0:c1], psBig[:, c0:c1]),
               [(sT, mk_bc1)], (sV, 1))
            mk_mu = sV.n
            yield
            for (a, b) in chunks(c0, c1, 512):
                MM(psBig[:, a:b], c1024[:], row_s2[0:1, a:b], True, True,
                   waits=[(sV, mk_mu)] if a == c0 else (),
                   inc=(sT, 1) if b == c1 else None)
            mk_bc2 = sT.n
            va = af[:, 0:cw]
            em("v", lambda e: e.tensor_mul(va, mu_b[:, c0:c1], mu_b[:, c0:c1]))
            em("v", lambda e: e.tensor_sub(va, psBig[:, c0:c1], va),
               [(sT, mk_bc2)], (sV, 1))
            mk_var = sV.n
            em("s", lambda e: e.activation(va, va, AFT.Ln, bias=eps_col[:]),
               [(sV, mk_var), (sM, mk_memset)], (sS, 1))
            em("s", lambda e: e.activation(inv_b[:, c0:c1], va, AFT.Exp, scale=-0.5),
               (), (sS, 1))
            mk_inv = sS.n
            for rn in rnames:
                set_free(rn, (sV, mk_var))
            yield
            return [(sV, mk_mu), (sS, mk_inv)]

        def ln_apply(src_of, dst_of, scale_of, bias_of, mk_stats, extra_w=(), cr=None):
            c0, c1 = cr if cr is not None else (0, T)
            t1 = tmp1[:, c0:c1]
            marks = []
            prev_s = 0
            for dc in range(ND):
                w = (list(mk_stats) + list(extra_w)) if dc == 0 else [(sS, prev_s)]
                em("v", lambda e, x_=src_of(dc), t1=t1:
                   e.tensor_sub(t1, x_[:, c0:c1], mu_b[:, c0:c1]), w)
                em("v", lambda e, t1=t1: e.tensor_mul(t1, t1, inv_b[:, c0:c1]),
                   (), (sV, 1))
                mkt = sV.n
                if scale_of is None:
                    em("s", lambda e, d_=dst_of(dc), t1=t1: e.copy(d_[:, c0:c1], t1),
                       [(sV, mkt)], (sS, 1))
                else:
                    em("s", lambda e, d_=dst_of(dc), sc=scale_of(dc), bi=bias_of(dc),
                       t1=t1:
                       e.activation(d_[:, c0:c1], t1, AFT.Identity, bias=bi, scale=sc),
                       [(sV, mkt)], (sS, 1))
                prev_s = sS.n
                marks.append(prev_s)
                yield
            return marks

        mk1 = drive(ln_stats(xc, [], w_dc=lambda dc: [(sX, mk_x[dc])]))
        mk_h1 = drive(ln_apply(xc, hc,
                               lambda dc: mod_sb[:, 8 + dc:9 + dc],
                               lambda dc: mod_sb[:, dc:dc + 1], mk1,
                               extra_w=[(sV, mk_mod)]))
        mk2 = drive(ln_stats(hc, [], w_dc=lambda dc: [(sS, mk_h1[dc])]))
        mk_xn = drive(ln_apply(hc, hc,
                               lambda dc: gam_sb[:, dc:dc + 1],
                               lambda dc: bet_sb[:, dc:dc + 1], mk2))

        gate(2)
        tap("d_xn", hx[:], [(sS, mk_xn[-1])])
        for kc in range(NQT):
            em("g", lambda e, kc=kc: e.memset(
                vaug(kc).rearrange("p (a b) -> p a b", b=192)[:, :, 64:128], 1.0),
                [(sT, mk_modmm)] if kc == 0 else (), (sM, 1))
        mk_vones = sM.n
        if upto == 30:
            B.stopped = True

        mk_vtok_all = 0
        for tt in range(NQT):
            rn = "BigL" if tt % 2 == 0 else "BigH"
            reg = psBig[:, 0:1024] if tt % 2 == 0 else psBig[:, 1024:2048]
            w0 = [(sWV, mk_wv[-1]), (sS, mk_xn[-1])] + free_of(rn)
            for dc in range(ND):
                for (a, b) in chunks(0, D, 512):
                    MM(reg[:, a:b],
                       hx[:, dc * T + tt * 128: dc * T + (tt + 1) * 128],
                       arena[:, o_wv + dc * 1024 + a: o_wv + dc * 1024 + b],
                       start=(dc == 0), stop=(dc == ND - 1),
                       waits=w0 if (dc == 0 and a == 0) else (),
                       inc=(sT, 1) if (dc == ND - 1 and b == D) else None)
            mkv = sT.n
            em("v", lambda e, tt=tt, reg=reg: e.tensor_copy(
                vaug(tt).rearrange("p (a b) -> p a b", b=192)[:, :, 0:64],
                reg.rearrange("p (a b) -> p a b", b=128)[:, :, 0:64]),
               [(sT, mkv), (sM, mk_vones)], (sV, 1))
            em("v", lambda e, tt=tt, reg=reg: e.tensor_copy(
                vaug(tt).rearrange("p (a b) -> p a b", b=192)[:, :, 128:192],
                reg.rearrange("p (a b) -> p a b", b=128)[:, :, 64:128]), (), (sV, 1))
            set_free(rn, (sV, sV.n))
            mk_vtok_all = sV.n
            if upto == 31 and tt >= 1:
                B.stopped = True
            if upto == 32 and tt >= 2:
                B.stopped = True

        gate(3)
        mk_vtok_mm = sT.n
        for j, kb in enumerate((KaA, KaB, KaC)):
            em("g", lambda e, kb=kb: e.memset(kb[64:65, :], 1.0),
               [(sT, mk_vtok_mm)] if j == 0 else (), (sM, 1))
        mk_kones = sM.n

        PA = [psBig[:, 1024:1536], psBig[:, 1536:2048]]
        PAn = ["PA0", "PA1"]
        for nm in PAn:
            set_free(nm, *FREE.get("BigH", []))
        set_free("CTX", *FREE.get("BigL", []))

        wqk_mk = {}
        wqk_free = {}
        for i in range(2):
            wqk_mk[i] = gdma(sQK[i % 2], wqk_buf(i), wqkT.ap()[i // 2, i % 2],
                             waits=[(sT, sT.n)] if i == 0 else ())
        mk_qm = {}
        mk_mrow_h = {}
        FREEPT = {0: (sT, 0), 1: (sT, 0)}
        AFFREE = {0: (sV, 0), 1: (sV, 0)}
        AVG = [0]
        pa_ci = [0]
        evi = [0]
        STOPMK = {}

        def gen_qkv(p):
            for qk in range(2):
                i = p * 2 + qk
                of = qm_of if qk == 0 else ka_of
                dstM = [of(2 * p), of(2 * p + 1)]
                mkq = 0
                for q4 in range(T // 512):
                    ci = pa_ci[0]
                    pa_ci[0] += 1
                    rn = PAn[ci % 2]
                    reg = PA[ci % 2]
                    w0 = [(sQK[i % 2], wqk_mk[i]), (sS, mk_xn[-1])] + free_of(rn)
                    for dc in range(ND):
                        MM(reg[:, 0:512],
                           wqk_buf(i)[:, dc * 128:(dc + 1) * 128],
                           hx[:, dc * T + q4 * 512:dc * T + (q4 + 1) * 512],
                           start=(dc == 0), stop=(dc == ND - 1),
                           waits=w0 if dc == 0 else (),
                           inc=(sT, 1) if dc == ND - 1 else None)
                    mkq = sT.n
                    sl = slice(q4 * 512, (q4 + 1) * 512)
                    if ci % 2 == 0:
                        em("s", lambda e, d=dstM[0], sl=sl, reg=reg:
                           e.copy(d[0:64, sl], reg[0:64, 0:512]),
                           [(sT, mkq), (sM, mk_kones)], (sS, 1))
                        em("s", lambda e, d=dstM[1], sl=sl, reg=reg:
                           e.copy(d[0:64, sl], reg[64:128, 0:512]), (), (sS, 1))
                        set_free(rn, (sS, sS.n))
                    else:
                        em("v", lambda e, d=dstM[0], sl=sl, reg=reg:
                           e.tensor_copy(d[0:64, sl], reg[0:64, 0:512]),
                           [(sT, mkq), (sM, mk_kones)], (sV, 1))
                        em("v", lambda e, d=dstM[1], sl=sl, reg=reg:
                           e.tensor_copy(d[0:64, sl], reg[64:128, 0:512]), (), (sV, 1))
                        set_free(rn, (sV, sV.n))
                    yield
                wqk_free[i] = (sT, mkq)
                if i + 2 < 2 * NP:
                    i2 = i + 2
                    wqk_mk[i2] = gdma(sQK[i2 % 2], wqk_buf(i2),
                                      wqkT.ap()[i2 // 2, i2 % 2], waits=[wqk_free[i]])
            mk_qm[2 * p] = [(sS, sS.n), (sV, sV.n)]
            mk_qm[2 * p + 1] = mk_qm[2 * p]

        def gen_passA(h):
            p, hl = divmod(h, 2)
            first = [True]
            for qt in range(NQT):
                klen = min(T, ((qt * 128) // 512) * 512 + 512)
                c0A = (qt * 128 // 512) * 512
                dlt = qt * 128 - c0A
                for j, (a, b) in enumerate(chunks(0, klen, 512)):
                    ci = pa_ci[0]
                    pa_ci[0] += 1
                    rn = PAn[ci % 2]
                    reg = PA[ci % 2]
                    w = list(free_of(rn))
                    if first[0]:
                        w += mk_qm[h] + [(sC, mk_consts_g)]
                        first[0] = False
                    diagc = (a == c0A)
                    MM(reg[:, 0:b - a],
                       qm_of(h)[0:64, qt * 128:(qt + 1) * 128],
                       ka_of(h)[0:64, a:b], True, not diagc, waits=w,
                       inc=None if diagc else (sT, 1))
                    if diagc:
                        mo = 384 - dlt
                        MM(reg[:, 0:b - a], ident[:], maskC[:, mo:mo + (b - a)],
                           False, True, inc=(sT, 1))
                    mkmm = sT.n
                    col = hl * NQT + qt
                    tcol = 2 * NQT + hl
                    tgt = col if j == 0 else tcol
                    em("v", lambda e, reg=reg, tgt=tgt, w_=b - a:
                       e.reduce_max(m_all[:, tgt:tgt + 1], reg[:, 0:w_], axis=AX,
                                    negate=True),
                       [(sT, mkmm)], (sV, 1) if j == 0 else None)
                    if j > 0:
                        em("v", lambda e, col=col, tcol=tcol:
                           e.tensor_tensor(m_all[:, col:col + 1], m_all[:, col:col + 1],
                                           m_all[:, tcol:tcol + 1], ALU.min),
                           (), (sV, 1))
                    set_free(rn, (sV, sV.n))
                    yield
            for b4 in range(NQT // 4):
                ci = pa_ci[0]
                pa_ci[0] += 1
                rn = PAn[ci % 2]
                reg = PA[ci % 2]
                for k4 in range(4):
                    qt = b4 * 4 + k4
                    col = hl * NQT + qt
                    MM(reg[0:1, k4 * 128:(k4 + 1) * 128],
                       m_all[:, col:col + 1], ident[:], True, True,
                       waits=(free_of(rn) + [(sV, sV.n)]) if k4 == 0 else (),
                       inc=(sT, 1) if k4 == 3 else None)
                mkr = sT.n
                em("s", lambda e, h=h, b4=b4, reg=reg:
                   e.copy(qm_of(h)[64:65, b4 * 512:(b4 + 1) * 512], reg[0:1, 0:512]),
                   [(sT, mkr)], (sS, 1))
                set_free(rn, (sS, sS.n))
                yield
            mk_mrow_h[h] = sS.n

        def emit_av(h, half, pend):
            (kc, q0, c0, bcells, bi, mkexp) = pend
            H0 = half * 1024
            off = 192 * (h // 2) + (64 if h % 2 else 0)
            lo = bcells[0][0]
            ptb = pt[bi % 2]
            avw = [(sS, mkexp)]
            if kc == 0:
                avw += [(sV, AVG[0])] + free_of("CTX") + dbgj()
            for (u0, u1) in bcells:
                stop = (kc == min(NQT - 1, (u0 + 384) // 128))
                MM(psBig[:, u0 - H0:u1 - H0],
                   vaug(kc)[:, off:off + 128],
                   ptb[:, u0 - lo:u1 - lo],
                   start=(kc == 0), stop=stop,
                   waits=avw if u0 == lo else (),
                   inc=(sT, 1) if (stop or u1 == bcells[-1][1]) else None)
                if stop:
                    STOPMK[u0] = sT.n
            FREEPT[bi % 2] = (sT, sT.n)

        from collections import deque
        bgq = deque()

        def bg_step(n=1):
            while n > 0 and bgq:
                try:
                    next(bgq[0])
                    n -= 1
                except StopIteration:
                    bgq.popleft()

        def bg_drain():
            while bgq:
                try:
                    next(bgq[0])
                except StopIteration:
                    bgq.popleft()

        mk_ctx_last = 0
        for _ in gen_qkv(0):
            pass
        for _ in gen_passA(0):
            pass

        for h in range(2 * NP):
            p, hl = divmod(h, 2)
            bg_drain()
            assert h in mk_mrow_h
            if hl == 0:
                bgq.append(gen_passA(h + 1))
            elif p + 1 < NP:
                bgq.append(gen_qkv(p + 1))
                bgq.append(gen_passA(h + 1))
            bi = 0
            for half, (H0, H1) in enumerate(chunks(0, T, 1024)):
                HW = H1 - H0
                batches = []
                for kc in range(H1 // 128):
                    q0 = kc * 128
                    c0 = (q0 // 512) * 512
                    lo0 = max(c0, H0)
                    cells = chunks(lo0, H1, 512)
                    for bs in range(0, len(cells), 2):
                        batches.append((kc, q0, c0, cells[bs:bs + 2],
                                        bs == 0 and c0 >= H0))
                AVG[0] = mk_ctx_last
                pend = None
                for (kc, q0, c0, bcells, diag) in batches:
                    lo = bcells[0][0]
                    hi_ = bcells[-1][1]
                    wdt = hi_ - lo
                    rn2 = "B2f" if bi % 2 == 0 else "C2f"
                    reg = psB2 if bi % 2 == 0 else psC2
                    ptb = pt[bi % 2]
                    w = list(free_of(rn2))
                    if kc == 0 and half == 0:
                        w += [(sS, mk_mrow_h[h])]
                    ncell = len(bcells)
                    for idx, (u0, u1) in enumerate(bcells):
                        isd = diag and idx == 0
                        MM(reg[:, u0 - lo:u1 - lo],
                           ka_of(h)[:, kc * 128:(kc + 1) * 128],
                           qm_of(h)[:, u0:u1], True, not isd,
                           waits=w if u0 == lo else (),
                           inc=(sT, 1) if (u1 == hi_ and not isd) else None)
                        if isd:
                            dlt2 = q0 - c0
                            mo = 896 + 384 - dlt2
                            MM(reg[:, 0:u1 - lo], ident[:],
                               maskC[:, mo:mo + (u1 - lo)], False, True,
                               inc=(sT, 1) if ncell == 1 else None)
                    mksc = sT.n
                    wexp = [(sT, mksc), FREEPT[bi % 2]]
                    em("s", lambda e, ptb=ptb, reg=reg, wdt=wdt:
                       e.activation(ptb[:, 0:wdt], reg[:, 0:wdt], AFT.Exp, scale=ISQ),
                       wexp, (sS, 1))
                    mkexp = sS.n
                    set_free(rn2, (sS, mkexp))
                    if pend is not None:
                        emit_av(h, half, pend)
                        bg_step(2)
                    pend = (kc, q0, c0, bcells, bi, mkexp)
                    bi += 1
                if pend is not None:
                    emit_av(h, half, pend)
                mk_av = sT.n
                crows = psBig[0:64] if hl == 0 else psBig[64:128]
                drows = psBig[64:128] if hl == 0 else psBig[0:64]
                cs = ctx(p)[hl * 64:hl * 64 + 64, :]
                for (u0, u1) in chunks(0, HW, 512):
                    sl = evi[0] % 2
                    aslot = af[0:64, sl * 1024:sl * 1024 + 512]
                    tslot = af[0:64, sl * 1024 + 512:sl * 1024 + 1024]
                    em("s", lambda e, drows=drows, u0=u0, u1=u1, tslot=tslot:
                       e.activation(tslot, drows[:, u0:u1], AFT.Ln),
                       [(sT, STOPMK.get(H0 + u0, mk_av))] +
                       (dbgj() if u0 == 0 else []) + [AFFREE[sl]],
                       (sS, 1))
                    em("s", lambda e, aslot=aslot, tslot=tslot:
                       e.activation(aslot, tslot, AFT.Exp, scale=-1.0), (), (sS, 1))
                    mkrc = sS.n
                    em("v", lambda e, cs=cs, crows=crows, H0=H0, u0=u0, u1=u1,
                       aslot=aslot:
                       e.tensor_mul(cs[:, H0 + u0:H0 + u1], crows[:, u0:u1], aslot),
                       [(sS, mkrc)], (sV, 1))
                    AFFREE[sl] = (sV, sV.n)
                    evi[0] += 1
                mk_ctx_last = sV.n
                set_free("CTX", (sV, mk_ctx_last))
                bg_step(3)
        bg_drain()
        gate(5)
        tap("d_ctx", arena[:, o_ctx:o_ctx + ND * T], [(sV, sV.n), (sS, sS.n)])
        mk_attn_done = (sT.n, sV.n, sS.n)

        wo_mk = {}
        wo_free = {}
        for ec in range(min(2, ND)):
            wo_mk[ec] = gdma(sWO[ec % 2], wo_buf(ec), woT.ap()[ec],
                             waits=[(sT, mk_attn_done[0])])
        regW_name = ["BigL", "BigH", "C2w"]
        regW_ap = [psBig[:, 0:1024], psBig[:, 1024:2048], psC2[:, 0:1024]]
        set_free("BigL", *FREE.get("CTX", []))
        set_free("BigH", *(FREE.get("PA0", []) + FREE.get("PA1", [])))
        set_free("C2w", *FREE.get("C2f", []))
        ri = 0
        mk_x2_all = 0
        mk_x2 = {}
        for ec in range(ND):
            for th, (a, b) in enumerate(chunks(0, T, 1024)):
                rn = regW_name[ri % 3]
                reg = regW_ap[ri % 3]
                w0 = [(sWO[ec % 2], wo_mk[ec])] + free_of(rn) + \
                    ([(sV, mk_attn_done[1]), (sS, mk_attn_done[2])] if ri == 0 else [])
                for dc in range(ND):
                    for (u0, u1) in chunks(a, b, 512):
                        MM(reg[:, u0 - a:u1 - a],
                           wo_buf(ec)[:, dc * 128:(dc + 1) * 128],
                           ctx(dc)[:, u0:u1],
                           start=(dc == 0), stop=(dc == ND - 1),
                           waits=w0 if (dc == 0 and u0 == a) else (),
                           inc=(sT, 1) if (dc == ND - 1 and u1 == b) else None)
                mkw = sT.n
                wo_free[ec] = (sT, mkw)
                em("v", lambda e, ec=ec, a=a, b=b, reg=reg:
                   e.scalar_tensor_tensor(x_sb[:, ec * T + a:ec * T + b], reg[:, 0:b - a],
                                          mod_sb[:, 16 + ec:17 + ec],
                                          x_sb[:, ec * T + a:ec * T + b],
                                          ALU.mult, ALU.add),
                   [(sT, mkw)], (sV, 1))
                set_free(rn, (sV, sV.n))
                ri += 1
            if ec + 2 < ND:
                wo_mk[ec + 2] = gdma(sWO[ec % 2], wo_buf(ec + 2), woT.ap()[ec + 2],
                                     waits=[wo_free[ec]])
            mk_x2[ec] = sV.n
            mk_x2_all = sV.n

        gate(6)
        tap("d_x2", x_sb[:], [(sV, mk_x2_all)])
        NHALF = max(1, T // 1024)
        HWT = T // NHALF
        mk_h2_h = {}
        mk_comb_h = {}
        mk_xn2_h = {}
        mk_inv4_h = {}

        def gen_transition(hf):
            c0, c1 = hf * HWT, (hf + 1) * HWT
            rn3 = "BigL" if hf == 0 else "BigH"
            mk3h = yield from ln_stats(xc, [], w_dc=lambda dc: [(sV, mk_x2[dc])],
                                       cr=(c0, c1), wj0=[])
            mk_h2h = yield from ln_apply(xc, hc,
                                         lambda dc: mod_sb[:, 32 + dc:33 + dc],
                                         lambda dc: mod_sb[:, 24 + dc:25 + dc],
                                         mk3h, cr=(c0, c1))
            mk_h2_h[hf] = mk_h2h
            for dc in range(ND):
                for (a, b) in chunks(c0, c1, 512):
                    MM(psBig[0:8, a:b], wg_sb[:, dc * 8:(dc + 1) * 8],
                       hc(dc)[:, a:b], start=(dc == 0), stop=(dc == ND - 1),
                       waits=([(sS, mk_h2h[-1]), (sC, mk_consts_g)] + free_of(rn3))
                       if (dc == 0 and a == c0) else (),
                       inc=(sT, 1) if (dc == ND - 1 and b == c1) else None)
                yield
            mk_lgm = sT.n
            em("v", lambda e: e.tensor_copy(lg_sb[:, c0:c1], psBig[0:8, c0:c1]),
               [(sT, mk_lgm)], (sV, 1))
            mk_lge = sV.n
            set_free(rn3, (sV, mk_lge))
            yield
            tt0 = c0 // 128
            ntt = HWT // 128
            for k in range(ntt):
                tt = tt0 + k
                MM(psBig[:, c0 + k * 8:c0 + (k + 1) * 8],
                   lg_sb[0:8, tt * 128:(tt + 1) * 128], ident[0:8, 0:8], True, True,
                   waits=([(sV, mk_lge)] + free_of(rn3)) if k == 0 else (),
                   inc=(sT, 1) if k == ntt - 1 else None)
            mk_gt = sT.n
            em("v", lambda e: e.tensor_copy(gtok_sb[:, tt0 * 8:tt0 * 8 + ntt * 8],
                                            psBig[:, c0:c0 + ntt * 8]),
               [(sT, mk_gt)], (sV, 1))
            mk_gte = sV.n
            set_free(rn3, (sV, mk_gte))
            yield
            q0, q1 = tt0 * NE, (tt0 + ntt) * NE
            GT2h = lambda t_: t_[:, q0:q1]
            rr = lambda t_: t_[:, q0:q1].rearrange("p (a b) -> p a b", b=NE)
            m1s = g3_sb[:, tt0:tt0 + ntt]
            trs = g3_sb[:, NQT + tt0:NQT + tt0 + ntt]
            m2s = g3_sb[:, 2 * NQT + tt0:2 * NQT + tt0 + ntt]
            rm1 = g3_sb[:, tt0:tt0 + ntt].rearrange("p (a b) -> p a b", b=1)
            rtr = g3_sb[:, NQT + tt0:NQT + tt0 + ntt].rearrange("p (a b) -> p a b", b=1)
            rm2 = g3_sb[:, 2 * NQT + tt0:2 * NQT + tt0 + ntt].rearrange(
                "p (a b) -> p a b", b=1)
            GT, G1, G2, IX = gtok_sb, g1_sb, g2_sb, idxm_sb
            em("v", lambda e: e.reduce_max(m1s, rr(GT), axis=AX), [(sV, mk_gte)])
            for j in range(NE):
                em("v", lambda e, j=j: e.tensor_tensor(rr(G2)[:, :, j:j + 1],
                                                       rr(GT)[:, :, j:j + 1], rm1,
                                                       ALU.is_equal))
            em("v", lambda e: e.tensor_mul(GT2h(G2), GT2h(G2), GT2h(IX)))
            em("v", lambda e: e.tensor_scalar_add(GT2h(G2), GT2h(G2), 99.0))
            em("v", lambda e: e.tensor_reduce(trs, rr(G2), axis=AX, op=ALU.min))
            em("v", lambda e: e.tensor_scalar_add(trs, trs, -99.0))
            yield
            for j in range(NE):
                em("v", lambda e, j=j: e.tensor_tensor(rr(G2)[:, :, j:j + 1],
                                                       rr(IX)[:, :, j:j + 1], rtr,
                                                       ALU.is_equal))
            em("v", lambda e: e.scalar_tensor_tensor(GT2h(G1), GT2h(G2), -1e38,
                                                     GT2h(GT), ALU.mult, ALU.add))
            em("v", lambda e: e.reduce_max(m2s, rr(G1), axis=AX))
            yield
            for j in range(NE):
                em("v", lambda e, j=j: e.tensor_tensor(rr(G1)[:, :, j:j + 1],
                                                       rr(G1)[:, :, j:j + 1], rm2,
                                                       ALU.is_equal))
            em("v", lambda e: e.tensor_mul(GT2h(G1), GT2h(G1), GT2h(IX)))
            em("v", lambda e: e.tensor_scalar_add(GT2h(G1), GT2h(G1), 99.0))
            em("v", lambda e: e.tensor_reduce(trs, rr(G1), axis=AX, op=ALU.min))
            em("v", lambda e: e.tensor_scalar_add(trs, trs, -99.0))
            yield
            for j in range(NE):
                em("v", lambda e, j=j: e.tensor_tensor(rr(G1)[:, :, j:j + 1],
                                                       rr(IX)[:, :, j:j + 1], rtr,
                                                       ALU.is_equal))
            em("v", lambda e: e.tensor_sub(trs, m2s, m1s), (), (sV, 1))
            mk_gd = sV.n
            em("s", lambda e: e.activation(trs, trs, AFT.Exp), [(sV, mk_gd)], (sS, 1))
            mk_ge = sS.n
            em("v", lambda e: e.tensor_scalar_add(trs, trs, 1.0), [(sS, mk_ge)])
            em("v", lambda e: e.reciprocal(trs, trs))
            em("v", lambda e: e.tensor_scalar(m2s, trs, -1.0, 1.0, ALU.mult, ALU.add))
            yield
            for j in range(NE):
                em("v", lambda e, j=j: e.tensor_tensor(rr(G2)[:, :, j:j + 1],
                                                       rr(G2)[:, :, j:j + 1], rtr,
                                                       ALU.mult))
            for j in range(NE):
                em("v", lambda e, j=j: e.tensor_tensor(rr(G1)[:, :, j:j + 1],
                                                       rr(G1)[:, :, j:j + 1], rm2,
                                                       ALU.mult))
            em("v", lambda e: e.tensor_add(GT2h(G1), GT2h(G1), GT2h(G2)))
            em("v", lambda e: e.tensor_mul(GT2h(G1), GT2h(G1), GT2h(alpha_sb)),
               [(sC, mk_consts_g)])
            em("v", lambda e: e.tensor_copy(GT2h(G1), GT2h(G1)), (), (sV, 1))
            mk_comb_h[hf] = sV.n
            yield
            mk4h = yield from ln_stats(hc, [], w_dc=lambda dc: [(sS, mk_h2h[dc])],
                                       cr=(c0, c1), wj0=[])
            mk_inv4_h[hf] = mk4h
            mk_xn2_h[hf] = yield from ln_apply(hc, hc, None, None, mk4h,
                                               extra_w=[(sV, mk_lge)], cr=(c0, c1))

        drive(gen_transition(0))
        if NHALF > 1:
            bgq.append(gen_transition(1))
        else:
            mk_h2_h[1] = mk_h2_h[0]
            mk_comb_h[1] = mk_comb_h[0]
            mk_xn2_h[1] = mk_xn2_h[0]
            mk_inv4_h[1] = mk_inv4_h[0]
        gate(7)

        NT2 = TH // 512 if TH >= 512 else 1
        t2ch = chunks(0, TH, 512)
        gi1 = 0
        gi2 = 0
        gate(8)
        fc1_free = {}
        fc2_free = {}
        mk_hdn_free = sT.n
        mk_cbev_last = 0
        mk_combb_free = {0: (sV, 0), 1: (sV, 0)}
        mk_out_free = {0: (sO[0], 0), 1: (sO[1], 0)}
        y_last = 0
        regY_name = ["B2y", "C2y"]
        set_free("B2y", *(FREE.get("B2f", []) or [(sV, 0)]))
        set_free("C2y", *(FREE.get("C2f", []) or [(sV, 0)]))
        set_free("F1a", *(FREE.get("B2f", []) or [(sV, 0)]))
        set_free("F1b", *(FREE.get("C2f", []) or [(sV, 0)]))

        for pi in range(2):
            t0 = pi * TH
            for ei in range(NE):
                tt0, tt1 = t0 // 128, (t0 + TH) // 128
                for tt in range(tt0, tt1):
                    MM(psBig[0:1, (tt - tt0) * 128:(tt - tt0) * 128 + 128],
                       g1_sb[:, tt * NE + ei:tt * NE + ei + 1], ident[:], True, True,
                       waits=(free_of("BigL") + [(sV, mk_comb_h[pi])])
                       if tt == tt0 else (),
                       inc=(sT, 1) if tt == tt1 - 1 else None)
                mkct = sT.n
                em("v", lambda e: e.tensor_copy(rows_bf[0:1, 0:TH], psBig[0:1, 0:TH]),
                   [(sT, mkct)], (sV, 1))
                mk_crow = sV.n
                for (a, b) in t2ch:
                    MM(psBig[:, a:b], onesrow[:], rows_bf[0:1, a:b],
                       True, True,
                       waits=([(sV, mk_crow), mk_combb_free[0]]) if a == 0 else (),
                       inc=(sT, 1) if b == TH else None)
                mkcb = sT.n
                em("v", lambda e, ei=ei: e.tensor_copy(combb(ei), psBig[:, 0:TH]),
                   [(sT, mkcb)], (sV, 1))
                mk_cbev = sV.n
                set_free("BigL", (sV, mk_cbev))

                for dc in range(ND):
                    em("s", lambda e, ei=ei, dc=dc, t0=t0:
                       e.activation(hdn_ap(dc), hc(dc)[:, t0:t0 + TH], AFT.Identity,
                                    bias=lnb_sb[:, ei * ND + dc:ei * ND + dc + 1],
                                    scale=lnw_sb[:, ei * ND + dc:ei * ND + dc + 1]),
                       (([(sS, mk_xn2_h[pi][-1]), (sT, mk_hdn_free),
                          (sSM, mk_smalls)] + dbgj())
                        if dc == 0 else ()), (sS, 1))
                mk_hdn = sS.n
                if pi == 0 and ei == 0:
                    tap("d_hdn", arena[:, o_hdn:o_hdn + ND * TH], [(sS, mk_hdn)])

                for fc in range(NFC):
                    if fc % 2 == 0:
                        w = [fc1_free[gi1 - 2]] if gi1 >= 2 else \
                            [(sT, mk_attn_done[0]), (sV, mk_comb_h[0])]
                        mkd = gdma(sF1[gi1 % 2], fc1buf2(gi1),
                                   fc1T.ap()[ei, fc // 2], waits=w)
                    bg_step(1)
                    rn = "F1a" if fc % 2 == 0 else "F1b"
                    reg = psB2 if fc % 2 == 0 else psC2
                    w0 = [(sF1[gi1 % 2], mkd), (sS, mk_hdn)] + free_of(rn)
                    for dc in range(ND):
                        for (a, b) in t2ch:
                            MM(reg[:, a:b],
                               fc1buf2(gi1)[:, (fc % 2) * 1024 + dc * 128:(fc % 2) * 1024 + (dc + 1) * 128],
                               hdn_ap(dc)[:, a:b],
                               start=(dc == 0), stop=(dc == ND - 1),
                               waits=w0 if (dc == 0 and a == 0) else (),
                               inc=(sT, 1) if (dc == ND - 1 and b == TH) else None)
                    mkf1 = sT.n
                    if fc % 2 == 1:
                        fc1_free[gi1] = (sT, mkf1)
                        gi1 += 1
                    em("s", lambda e, ei=ei, fc=fc, reg=reg:
                       e.activation(A_ap(fc), reg[:, 0:TH], AFT.Gelu_apprx_tanh,
                                    bias=fc1b_sb[:, ei * NFC + fc:ei * NFC + fc + 1]),
                       [(sT, mkf1), (sC, mk_consts_g)] + (dbgj() if fc == 0 else []), (sS, 1))
                    set_free(rn, (sS, sS.n))
                mk_gelu = sS.n
                if pi == 0 and ei == 0:
                    tap("d_A", A_ap(0), [(sS, mk_gelu)])
                mk_hdn_free = sT.n

                for hcx in range(ND):
                    bg_step(1)
                    rn = regY_name[hcx % 2]
                    regY = psB2 if hcx % 2 == 0 else psC2
                    mkd = {}
                    for hf in range(2):
                        gih = gi2 * 2 + hf
                        w = [fc2_free[gih - 2]] if gih >= 2 else \
                            [(sT, mk_attn_done[0]), (sV, mk_comb_h[0])]
                        mkd[hf] = (sF2[gih % 2],
                                   gdma(sF2[gih % 2], fc2buf(gih),
                                        fc2T.ap()[ei, hcx][:, hf * 2048:(hf + 1) * 2048],
                                        waits=w))
                    w0 = [(sS, mk_gelu)] + free_of(rn) + \
                        (free_of("F1a") + free_of("F1b") if hcx <= 1 else [])
                    for fs in range(NFC):
                        hf = fs // 16
                        gih = gi2 * 2 + hf
                        wf = ([mkd[hf]] + (w0 if fs == 0 else []))
                        for (a, b) in t2ch:
                            MM(regY[:, a:b],
                               fc2buf(gih)[:, (fs % 16) * 128:(fs % 16) * 128 + 128],
                               A_ap(fs)[:, a:b],
                               start=(fs == 0), stop=(fs == NFC - 1),
                               waits=wf if (a == 0 and fs % 16 == 0) else (),
                               inc=(sT, 1) if (b == TH and fs % 16 == 15) else None)
                        if fs % 16 == 15:
                            fc2_free[gi2 * 2 + hf] = (sT, sT.n)
                    mkf2 = sT.n
                    bcol = fc2bc_sb[:, ei * ND + hcx:ei * ND + hcx + 1]
                    for (a, b) in t2ch:
                        if ei == 0:
                            em("v", lambda e, regY=regY, a=a, b=b, ei=ei, bcol=bcol, hcx=hcx:
                               e.scalar_tensor_tensor(acc_ap_l(hcx)[:, a:b], regY[:, a:b],
                                                      bcol, combb(ei)[:, a:b],
                                                      ALU.add, ALU.mult),
                               [(sT, mkf2), (sV, mk_cbev), (sSM, mk_smalls)], (sV, 1))
                        else:
                            em("v", lambda e, regY=regY, a=a, b=b, ei=ei, bcol=bcol:
                               e.scalar_tensor_tensor(af[:, 1024:1024 + b - a], regY[:, a:b],
                                                      bcol, combb(ei)[:, a:b],
                                                      ALU.add, ALU.mult),
                               [(sT, mkf2), (sV, mk_cbev)], None)
                            em("v", lambda e, hcx=hcx, a=a, b=b: e.tensor_add(
                                acc_ap_l(hcx)[:, a:b], acc_ap_l(hcx)[:, a:b],
                                af[:, 1024:1024 + b - a]),
                               (), (sV, 1))
                    set_free(rn, (sV, sV.n))
                    gi2 += 1
                mk_acc_w = sV.n
                mk_combb_free[0] = (sV, mk_acc_w)
                set_free("F1a", (sV, mk_acc_w))
                set_free("F1b", (sV, mk_acc_w))

            bg_drain()
            if pi == 0:
                tap("d_acc", arena[:, o_acc:o_acc + ND * TH], [(sV, mk_acc_w)])
            if upto <= 9 and pi == 0:
                B.stopped = True
            waf = [mk_inv4_h[1][1]] if (pi == 0 and NHALF > 1) else []
            for hcx in range(ND):
                for (a, b) in t2ch:
                    oi = hcx * max(1, NT2) + a // 512
                    em("v", lambda e, hcx=hcx, a=a, b=b, oi=oi, t0=t0: e.scalar_tensor_tensor(
                        out_st(oi)[:, 0:b - a], acc_ap(hcx)[:, a:b],
                        mod_sb[:, 40 + hcx:41 + hcx],
                        x_sb[:, hcx * T + t0 + a:hcx * T + t0 + b],
                        ALU.mult, ALU.add),
                       [(sV, mk_acc_w), mk_out_free[oi % 2]] +
                       (waf if (hcx == 0 and a == 0) else []), (sV, 1))
                    mko = sV.n
                    y_last = ydma(sO[oi % 2], outT.ap()[hcx][:, t0 + a:t0 + b],
                                  out_st(oi)[:, 0:b - a], waits=[(sV, mko)])
                    mk_out_free[oi % 2] = (sO[oi % 2], y_last)

        B.raw("y", lambda e: (e.wait_ge(sO[0].h, sO[0].n), e.wait_ge(sO[1].h, sO[1].n),
                              (e.wait_ge(sDB.h, sDB.n) if dbg else None)), ())

        B.check_deadlock()

        es.enter_context(nc.allow_low_precision(
            reason="bf16 activations with fp32 psum accumulation; fp32 stats math"))
        with nc.Block() as blk:
            @blk.tensor
            def _(e):
                for f in B.streams["t"]:
                    f(e)

            @blk.vector
            def _(e):
                for f in B.streams["v"]:
                    f(e)

            @blk.scalar
            def _(e):
                for f in B.streams["s"]:
                    f(e)

            @blk.gpsimd
            def _(e):
                for f in B.streams["g"]:
                    f(e)

            @blk.sync
            def _(e):
                for f in B.streams["y"]:
                    f(e)

    return nc



def prep_inputs(b, T, L0_attn_x, L0_attn_ln_gamma, L0_attn_ln_beta, L0_attn_W_qkv,
                L0_attn_W_o, L0_ffn_Wg, L0_ffn_alpha, L0_ffn_ln_w, L0_ffn_ln_b,
                L0_ffn_fc1_w, L0_ffn_fc1_b, L0_ffn_fc2_w, L0_ffn_fc2_b, L0_c, L0_wmod):
    NQT = T // 128
    f = np.float32
    d = {}
    d["xT"] = np.ascontiguousarray(np.asarray(L0_attn_x[b], f)[:T].T.reshape(ND, 128, T))
    d["c_col"] = np.ascontiguousarray(np.asarray(L0_c[b], f).reshape(ND, 128).T)
    wm = np.asarray(L0_wmod, f).reshape(ND, 128, 48, 128)
    d["wmodT"] = np.ascontiguousarray(wm.transpose(1, 2, 0, 3).reshape(128, 48 * ND * 128))
    d["gamma_c"] = np.ascontiguousarray(np.asarray(L0_attn_ln_gamma, f).reshape(ND, 128).T)
    d["beta_c"] = np.ascontiguousarray(np.asarray(L0_attn_ln_beta, f).reshape(ND, 128).T)
    W3 = np.asarray(L0_attn_W_qkv, f).reshape(NH, 3, DH, D)
    wqk = np.zeros((NP, 2, 128, ND * 128), f)
    for p in range(NP):
        for qk in range(2):
            blkw = np.concatenate([W3[2 * p, qk], W3[2 * p + 1, qk]], axis=0)
            wqk[p, qk] = blkw.T.reshape(ND, 128, 128).transpose(1, 0, 2).reshape(128, ND * 128)
    d["wqkT"] = wqk
    Vmov = W3[:, 2].reshape(D, D).T
    d["wvT"] = np.ascontiguousarray(Vmov.reshape(ND, 128, D))
    woL = np.asarray(L0_attn_W_o, f).T
    d["woT"] = np.ascontiguousarray(
        woL.reshape(ND, 128, ND, 128).transpose(2, 1, 0, 3).reshape(ND, 128, ND * 128))
    d["wgT"] = np.ascontiguousarray(
        np.asarray(L0_ffn_Wg, f).reshape(ND, 128, NE).transpose(1, 0, 2).reshape(128, ND * NE))
    d["alpha_tok"] = np.ascontiguousarray(np.tile(np.asarray(L0_ffn_alpha, f), (128, NQT)))
    d["lnw_c"] = np.ascontiguousarray(
        np.asarray(L0_ffn_ln_w, f).reshape(NE, ND, 128).transpose(2, 0, 1).reshape(128, NE * ND))
    d["lnb_c"] = np.ascontiguousarray(
        np.asarray(L0_ffn_ln_b, f).reshape(NE, ND, 128).transpose(2, 0, 1).reshape(128, NE * ND))
    fc1 = np.asarray(L0_ffn_fc1_w, f)
    d["fc1T"] = np.ascontiguousarray(
        fc1.reshape(NE, ND, 128, NFC // 2, 2, 128).transpose(0, 3, 2, 4, 1, 5)
        .reshape(NE, NFC // 2, 128, 2 * ND * 128))
    d["fc1b_c"] = np.ascontiguousarray(
        np.asarray(L0_ffn_fc1_b, f).reshape(NE, NFC, 128).transpose(2, 0, 1).reshape(128, NE * NFC))
    fc2 = np.asarray(L0_ffn_fc2_w, f)
    d["fc2T"] = np.ascontiguousarray(
        fc2.reshape(NE, NFC, 128, ND, 128).transpose(0, 3, 2, 1, 4).reshape(NE, ND, 128, NFC * 128))
    d["fc2b_cD"] = np.ascontiguousarray(
        np.asarray(L0_ffn_fc2_b, f).reshape(NE, ND, 128).transpose(2, 0, 1).reshape(128, NE * ND))
    d["identF"] = np.eye(128, dtype=f)
    uu = np.arange(896)[None, :]
    pp = np.arange(128)[:, None]
    mA = np.where((uu - 384) <= pp, 0.0, -1e38).astype(f)
    mB = np.where((uu - 384) >= pp, 0.0, -1e38).astype(f)
    d["maskC_F"] = np.stack([mA, mB])
    d["idxmF"] = np.ascontiguousarray(np.tile(np.arange(NE, dtype=f) - 99.0, (128, NQT)))
    return d


def kernel(**inputs):
    T = 2048
    nc = build_nc(T)
    in_maps = [prep_inputs(b, T, **inputs) for b in range(8)]
    res = run_bass_kernel_spmd(nc, in_maps, core_ids=list(range(8)))
    outs = []
    for r in res.results:
        o = np.asarray(r["outT"], dtype=np.float32).reshape(D, T)
        outs.append(o.T)
    return np.stack(outs, axis=0)


# version 14 (final)
# speedup vs baseline: 1.1107x; 1.1107x over previous
import numpy as np
import ml_dtypes
import concourse.bass as bass
import concourse.mybir as mybir
from concourse.bass_utils import run_bass_kernel_spmd

F32 = mybir.dt.float32
BF16 = mybir.dt.bfloat16
AX = mybir.AxisListType.X
AFT = mybir.ActivationFunctionType
ALU = mybir.AluOpType

D = 1024
ND = 8
NH = 16
DH = 64
NP = 8
NE = 8
FF = 4096
NFC = 32
EPS = 1e-5
ISQ = 0.125


class Sem:
    __slots__ = ("h", "n")

    def __init__(self, h):
        self.h = h
        self.n = 0


def chunks(lo, hi, step):
    out = []
    while lo < hi:
        out.append((lo, min(lo + step, hi)))
        lo += step
    return out


class Builder:
    def __init__(self):
        self.streams = {k: [] for k in ("t", "v", "s", "g", "y")}
        self.meta = []
        self.selfsem = {}
        self.stopped = False

    def em(self, eng, op, waits=(), inc=None):
        if self.stopped:
            return
        wl = [(s, v) for (s, v) in waits if v > 0]
        ss = self.selfsem.get(eng)
        if ss is not None:
            if ss.n > 0:
                wl.append((ss, ss.n))
            inc = (ss, 1) if inc is None else inc
            assert inc[0] is ss, "self-serialized engine must inc its own sem"
        incv = None
        if inc is not None:
            sem, n = inc
            sem.n += n
            incv = (sem, n)
        self.meta.append((eng, wl, incv))

        def f(e, op=op, wl=tuple(wl), incv=incv):
            for (s, v) in wl:
                e.wait_ge(s.h, v)
            ins = op(e)
            if incv is not None:
                ins.then_inc(incv[0].h, incv[1])
        self.streams[eng].append(f)

    def raw(self, eng, fn, waits=()):
        if self.stopped:
            return
        wl = [(s, v) for (s, v) in waits if v > 0]
        self.meta.append((eng, wl, None))

        def f(e, fn=fn, wl=tuple(wl)):
            for (s, v) in wl:
                e.wait_ge(s.h, v)
            fn(e)
        self.streams[eng].append(f)

    def check_deadlock(self):
        by_eng = {k: [] for k in self.streams}
        for (eng, waits, incv) in self.meta:
            by_eng[eng].append((waits, incv))
        ptr = {k: 0 for k in by_eng}
        counts = {}
        progressed = True
        while progressed:
            progressed = False
            for k in by_eng:
                while ptr[k] < len(by_eng[k]):
                    waits, incv = by_eng[k][ptr[k]]
                    if all(counts.get(id(s), 0) >= v for (s, v) in waits):
                        if incv is not None:
                            s, n = incv
                            counts[id(s)] = counts.get(id(s), 0) + n
                        ptr[k] += 1
                        progressed = True
                    else:
                        break
        stuck = {k: ptr[k] for k in by_eng if ptr[k] < len(by_eng[k])}
        if stuck:
            det = []
            for k in stuck:
                waits, _ = by_eng[k][ptr[k]]
                det.append(f"{k}@{ptr[k]}/{len(by_eng[k])} needs "
                           f"{[(v, counts.get(id(s), 0)) for s, v in waits]}")
            raise RuntimeError("emitted program deadlocks: " + "; ".join(det))


class _StopEmit(Exception):
    pass


def build_nc(T=2048, dbg=False, upto=99):
    nc = bass.Bass("TRN2", target_bir_lowering=False)
    NQT = T // 128
    TH = T // 2
    B = Builder()
    em = B.em

    def din(name, shape):
        return nc.dram_tensor(name, list(shape), F32, kind="ExternalInput")

    xT = din("xT", (ND, 128, T))
    c_col = din("c_col", (128, ND))
    wmodT = din("wmodT", (128, 48 * ND * 128))
    gamma_c = din("gamma_c", (128, ND))
    beta_c = din("beta_c", (128, ND))
    wqkT = din("wqkT", (NP, 2, 128, ND * 128))
    wvT = din("wvT", (ND, 128, D))
    woT = din("woT", (ND, 128, ND * 128))
    wgT = din("wgT", (128, ND * NE))
    alpha_tok = din("alpha_tok", (128, NQT * NE))
    lnw_c = din("lnw_c", (128, NE * ND))
    lnb_c = din("lnb_c", (128, NE * ND))
    fc1T = din("fc1T", (NE, NFC // 2, 128, 2 * ND * 128))
    fc1b_c = din("fc1b_c", (128, NE * NFC))
    fc2T = din("fc2T", (NE, ND, 128, NFC * 128))
    fc2b_cD = din("fc2b_cD", (128, NE * ND))
    identF = din("identF", (128, 128))
    maskC_F = din("maskC_F", (2, 128, 896))
    idxmF = din("idxmF", (128, (T // 128) * NE))
    outT = nc.dram_tensor("outT", [ND, 128, T], F32, kind="ExternalOutput")
    dbg_t = {}
    if dbg:
        for nm, sh in (("d_mod", (128, 48)), ("d_xn", (128, ND * T)),
                       ("d_ctx", (128, ND * T)), ("d_x2", (128, ND * T)),
                       ("d_xn2", (128, ND * T)), ("d_comb", (128, (T // 128) * NE)),
                       ("d_lg", (NE, T)), ("d_hdn", (128, ND * (T // 2))),
                       ("d_A", (128, T // 2)), ("d_acc", (128, ND * (T // 2)))):
            dbg_t[nm] = nc.dram_tensor(nm, list(sh), F32, kind="ExternalOutput")

    from contextlib import ExitStack
    with ExitStack() as es:
        sbuf = lambda n, sh, dt: es.enter_context(nc.sbuf_tensor(n, list(sh), dt))
        psum = lambda n, sh, dt: es.enter_context(nc.psum_tensor(n, list(sh), dt))

        x_sb = sbuf("x_sb", (128, ND * T), BF16)
        hx = sbuf("hx", (128, ND * T), BF16)
        mu_b = sbuf("mu_b", (128, T), BF16)
        inv_b = sbuf("inv_b", (128, T), BF16)
        rows_bf = sbuf("rows_bf", (1, T), BF16)
        m_all = sbuf("m_all", (128, 2 * NQT + 2), BF16)
        mod_sb = sbuf("mod_sb", (128, 48), F32)
        ident = sbuf("ident", (128, 128), BF16)
        identn = sbuf("identn", (128, 128), BF16)
        maskC = sbuf("maskC", (128, 2 * 896), BF16)
        onescol = sbuf("onescol", (128, 1), BF16)
        onesrow = sbuf("onesrow", (1, 128), BF16)
        c1024 = sbuf("c1024", (1, 128), BF16)
        eps_col = sbuf("eps_col", (128, 1), F32)
        tiny_col = sbuf("tiny_col", (128, 1), F32)
        ccol_sb = sbuf("ccol_sb", (128, ND), F32)
        siluc_sb = sbuf("siluc_sb", (128, ND), BF16)
        gam_sb = sbuf("gam_sb", (128, ND), F32)
        bet_sb = sbuf("bet_sb", (128, ND), F32)
        lnw_sb = sbuf("lnw_sb", (128, NE * ND), F32)
        lnb_sb = sbuf("lnb_sb", (128, NE * ND), F32)
        fc1b_sb = sbuf("fc1b_sb", (128, NE * NFC), BF16)
        wg_sb = sbuf("wg_sb", (128, ND * NE), BF16)
        alpha_sb = sbuf("alpha_sb", (128, NQT * NE), BF16)
        fc2bc_sb = sbuf("fc2bc_sb", (128, NE * ND), F32)
        gtok_sb = sbuf("gtok_sb", (128, NQT * NE), BF16)
        g1_sb = sbuf("g1_sb", (128, NQT * NE), BF16)
        g2_sb = sbuf("g2_sb", (128, NQT * NE), BF16)
        g3_sb = sbuf("g3_sb", (128, 3 * NQT), F32)
        idxm_sb = sbuf("idxm_sb", (128, NQT * NE), BF16)

        o_vaug = 0
        o_ctx = o_vaug + NQT * 1536
        o_QmA = o_ctx + ND * T
        o_QmB = o_QmA + T
        o_KaA = o_QmB + T
        o_KaB = o_KaA + T
        o_QmC = o_KaB + T
        o_KaC = o_QmC + T
        AR1 = o_KaC + T
        o_A = 0
        o_hdn = o_A + NFC * TH
        o_fc2 = o_hdn + ND * TH
        o_fc1 = o_fc2 + 2 * 2048
        o_acc = o_fc1 + 2 * 2048
        if T >= 2048:
            ARN = max(o_acc + ND * TH, AR1, 48 * ND * 128 + 0)
            o_wv = o_QmA
            o_wo = o_acc
            o_wqk = AR1
            assert o_wv + ND * 1024 <= ARN and o_wqk + 2 * 1024 <= ARN
        else:
            base2 = max(AR1, o_acc + ND * TH)
            o_wqk = base2
            o_wo = base2 + 2 * 1024
            o_wv = base2 + 4 * 1024
            ARN = max(o_wv + ND * 1024, 48 * ND * 128)
        arena = sbuf("arena", (128, ARN), BF16)
        af = sbuf("af", (128, max(T, 3 * 512)), F32)
        combb_sb = sbuf("combb_sb", (128, max(TH, 512)), BF16)
        o_out = 0
        row_f = af[0:1, 0:T]
        o_wmod = 0

        lg_sb = arena[0:NE, o_ctx:o_ctx + T]
        tmp1 = arena[:, o_ctx + T:o_ctx + 2 * T]
        row_s1 = arena[0:1, o_ctx + 2 * T:o_ctx + 3 * T]
        row_s2 = arena[0:1, o_ctx + 3 * T:o_ctx + 4 * T]
        vaug = lambda kc: arena[:, o_vaug + kc * 1536: o_vaug + (kc + 1) * 1536]
        ctx = lambda p: arena[:, o_ctx + p * T: o_ctx + (p + 1) * T]
        QmA = arena[0:65, o_QmA:o_QmA + T]
        QmB = arena[0:65, o_QmB:o_QmB + T]
        QmC = arena[0:65, o_QmC:o_QmC + T]
        KaA = arena[0:65, o_KaA:o_KaA + T]
        KaB = arena[0:65, o_KaB:o_KaB + T]
        KaC = arena[0:65, o_KaC:o_KaC + T]

        def qm_of(h):
            p, hl = divmod(h, 2)
            return QmA if hl == 0 else (QmB if p % 2 == 0 else QmC)

        def ka_of(h):
            p, hl = divmod(h, 2)
            return KaA if hl == 0 else (KaB if p % 2 == 0 else KaC)
        pt = [mu_b[:, 0:min(T, 1024)], (mu_b[:, 1024:2048] if T >= 2048 else inv_b[:, 0:min(T, 1024)])]
        wqk_buf = lambda i: arena[:, o_wqk + (i % 2) * 1024: o_wqk + (i % 2) * 1024 + 1024]
        wo_buf = lambda i: arena[:, o_wo + (i % 2) * 1024: o_wo + (i % 2) * 1024 + 1024]
        A_ap = lambda fc: arena[:, o_A + fc * TH: o_A + (fc + 1) * TH]
        hdn_ap = lambda dc: arena[:, o_hdn + dc * TH: o_hdn + (dc + 1) * TH]
        fc1buf2 = lambda i: arena[:, o_fc1 + (i % 2) * 2048: o_fc1 + (i % 2) * 2048 + 2048]
        fc2buf = lambda i: arena[:, o_fc2 + (i % 2) * 2048: o_fc2 + (i % 2) * 2048 + 2048]
        acc_ap = lambda hcx: arena[:, o_acc + hcx * TH: o_acc + (hcx + 1) * TH]
        acc_ap_l = acc_ap
        combb = lambda i: combb_sb[:, 0:TH]
        out_st = lambda i: af[:, o_out + (i % 2) * 512: o_out + (i % 2) * 512 + 512]

        psBig = psum("psBig", (128, 2048), F32)
        psB2 = psum("psB2", (128, 1024), F32)
        psC2 = psum("psC2", (128, 1024), F32)

        mk_sem = lambda n: Sem(es.enter_context(nc.semaphore(n)))
        sT, sV, sS, sM = mk_sem("sT"), mk_sem("sV"), mk_sem("sS"), mk_sem("sM")
        sX, sC, sW0, sWV = mk_sem("sX"), mk_sem("sC"), mk_sem("sW0"), mk_sem("sWV")
        sQK = [mk_sem("sQK0"), mk_sem("sQK1")]
        sWO = [mk_sem("sWO0"), mk_sem("sWO1")]
        sF1 = [mk_sem("sF10"), mk_sem("sF11")]
        sF2 = [mk_sem("sF20"), mk_sem("sF21")]
        sSM = mk_sem("sSM")
        sO = [mk_sem("sO0"), mk_sem("sO1")]
        B.selfsem = {"v": sV, "s": sS}
        sDB = mk_sem("sDB")

        def dbgj():
            return [(sV, sV.n)] if dbg else []

        def gate(k):
            if upto < k:
                B.stopped = True

        def tap(nm, src, waits=()):
            if not dbg or (dbg is not True and nm not in dbg):
                return
            total = dbg_t[nm].shape[-1]
            csz = min(total, 512)
            buf = es.enter_context(nc.sbuf_tensor("b_" + nm, [dbg_t[nm].shape[0], csz], F32))
            for j, (a, b) in enumerate(chunks(0, total, csz)):
                em("v", lambda e, a=a, b=b, buf=buf, s_=src:
                   e.tensor_copy(buf[:, 0:b - a], s_[:, a:b]),
                   list(waits) + ([(sDB, sDB.n)] if j else []), (sV, 1))
                em("y", lambda e, a=a, b=b, buf=buf, nm=nm:
                   e.dma_start(dbg_t[nm].ap()[:, a:b], buf[:, 0:b - a]),
                   [(sV, sV.n)], (sDB, 16))

        FREE = {}

        def free_of(name):
            return list(FREE.get(name, ()))

        def set_free(name, *marks):
            FREE[name] = [(s, v) for (s, v) in marks]

        def add_free(name, sem, val):
            FREE.setdefault(name, []).append((sem, val))

        def gdma(sem, dst, src, waits=()):
            em("g", lambda e, d=dst, s=src: e.dma_start(d, s), waits, (sem, 16))
            return sem.n

        def ydma(sem, dst, src, waits=()):
            em("y", lambda e, d=dst, s=src: e.dma_start(d, s), waits, (sem, 16))
            return sem.n

        def MM(out, lhsT, rhs, start, stop, waits=(), inc=None, tp=None):
            def op(e, o=out, l=lhsT, r=rhs, st=start, sp=stop, tp=tp):
                kw = {"skip_group_check": True}
                if tp is not None:
                    kw["tile_position"] = tp
                return e.matmul(o, l, r, start=st, stop=sp, **kw)
            em("t", op, waits, inc)

        for ap_, val in ((onescol[:], 1.0), (onesrow[:], 1.0), (c1024[:], 1.0 / D),
                         (eps_col[:], float(EPS)), (tiny_col[:], 1e-30)):
            em("g", lambda e, a=ap_, v=val: e.memset(a, v), (), (sM, 1))
        mk_memset = sM.n

        for dst, src in ((ccol_sb[:], c_col.ap()), (gam_sb[:], gamma_c.ap()),
                         (bet_sb[:], beta_c.ap()), (lnw_sb[:], lnw_c.ap()),
                         (lnb_sb[:], lnb_c.ap()),
                         (fc2bc_sb[:], fc2b_cD.ap())):
            ydma(sSM, dst, src)
        mk_smalls = sSM.n
        gdma(sC, alpha_sb[:], alpha_tok.ap())
        gdma(sC, ident[:], identF.ap())
        gdma(sC, maskC[:, 0:896], maskC_F.ap()[0])
        gdma(sC, maskC[:, 896:2 * 896], maskC_F.ap()[1])
        gdma(sC, wg_sb[:], wgT.ap())
        gdma(sC, idxm_sb[:], idxmF.ap())
        gdma(sC, fc1b_sb[:], fc1b_c.ap())
        mk_consts_g = sC.n
        em("v", lambda e: e.tensor_scalar_mul(identn[:], ident[:], -1.0),
           [(sC, mk_consts_g)], (sV, 1))
        mk_x = []
        for dc in range(ND):
            mk_x.append(gdma(sX, x_sb[:, dc * T:(dc + 1) * T], xT.ap()[dc]))
        for i6 in range(8):
            w_ = 48 * ND * 128 // 8
            gdma(sW0, arena[:, o_wmod + i6 * w_:o_wmod + (i6 + 1) * w_],
                 wmodT.ap()[:, i6 * w_:(i6 + 1) * w_])
        mk_wmod = sW0.n
        mk_wv = []

        em("s", lambda e: e.activation(siluc_sb[:], ccol_sb[:], AFT.Silu),
           [(sSM, mk_smalls)], (sS, 1))
        mk_silu = sS.n

        for mc in range(48):
            for dc in range(ND):
                MM(psC2[:, mc:mc + 1],
                   arena[:, o_wmod + (mc * ND + dc) * 128:o_wmod + (mc * ND + dc) * 128 + 128],
                   siluc_sb[:, dc:dc + 1], start=(dc == 0), stop=(dc == ND - 1),
                   waits=([(sW0, mk_wmod), (sS, mk_silu)] if (mc == 0 and dc == 0) else ()),
                   inc=((sT, 1) if (mc == 47 and dc == ND - 1) else None))
        mk_modmm = sT.n
        for dc in range(ND):
            mk_wv.append(gdma(sWV, arena[:, o_wv + dc * 1024:o_wv + (dc + 1) * 1024],
                              wvT.ap()[dc], waits=[(sT, mk_modmm)] if dc == 0 else ()))
        em("v", lambda e: e.tensor_copy(mod_sb[:], psC2[:, 0:48]), [(sT, mk_modmm)], (sV, 1))
        mk_modev = sV.n
        set_free("C2a", (sV, mk_modev))
        set_free("C2b", (sV, mk_modev))
        em("v", lambda e: e.tensor_scalar_add(mod_sb[:, 8:16], mod_sb[:, 8:16], 1.0),
           [(sV, mk_modev)])
        em("v", lambda e: e.tensor_scalar_add(mod_sb[:, 32:40], mod_sb[:, 32:40], 1.0))
        for base in (16, 40):
            sl = mod_sb[:, base:base + 8]
            t_ = g1_sb[:, 0:8]
            s_ = g2_sb[:, 0:8]
            em("s", lambda e, sl=sl, s_=s_: e.activation(s_, sl, AFT.Sign, bias=tiny_col[:]),
               [(sV, sV.n), (sM, mk_memset)], (sS, 1))
            mk_sgn = sS.n
            em("v", lambda e, sl=sl, t_=t_, s_=s_: e.tensor_mul(t_, sl, s_), [(sS, mk_sgn)])
            em("v", lambda e, t_=t_: e.tensor_scalar_max(t_, t_, 0.25))
            em("v", lambda e, sl=sl, t_=t_, s_=s_: e.tensor_mul(sl, t_, s_), (), (sV, 1))
        mk_mod = sV.n
        tap("d_mod", mod_sb[:])
        gate(1)

        xc = lambda dc: x_sb[:, dc * T:(dc + 1) * T]
        hc = lambda dc: hx[:, dc * T:(dc + 1) * T]

        def ln_stats(src_of, w_first, w_dc=None):
            wj = list(w_first) + [(sV, sV.n), (sS, sS.n)]
            wd = (lambda dc: []) if w_dc is None else w_dc
            for dc in range(ND):
                for (a, b) in chunks(0, T, 512):
                    w = list(wd(dc)) if a == 0 else []
                    if dc == 0 and a == 0:
                        w += wj + [(sM, mk_memset)] + free_of("BigL") + free_of("BigH")
                    MM(psBig[0:1, a:b], onescol[:], src_of(dc)[:, a:b],
                       start=(dc == 0), stop=(dc == ND - 1), waits=w,
                       inc=((sT, 1) if (dc == ND - 1 and b == T) else None))
            mk_s1 = sT.n
            em("v", lambda e: e.tensor_copy(row_s1[:, 0:T], psBig[0:1, 0:T]),
               [(sT, mk_s1)], (sV, 1))
            mk_s1e = sV.n
            mm_dc = {}
            for dc in range(ND):
                sq = mu_b if dc % 2 == 0 else inv_b
                sw = list(wd(dc)) + ([(sT, mm_dc[dc - 2])] if dc >= 2 else list(wj))
                em("s", lambda e, s_=sq, x_=src_of(dc): e.activation(s_[:], x_, AFT.Square),
                   sw, (sS, 1))
                mkq = sS.n
                for (a, b) in chunks(0, T, 512):
                    MM(psBig[0:1, a:b], onescol[:], sq[:, a:b],
                       start=(dc == 0), stop=(dc == ND - 1),
                       waits=([(sS, mkq)] + ([(sV, mk_s1e)] if dc == 0 else []))
                       if a == 0 else (),
                       inc=(sT, 1) if b == T else None)
                mm_dc[dc] = sT.n
            mk_s2 = sT.n
            em("v", lambda e: e.tensor_copy(row_s2[:, 0:T], psBig[0:1, 0:T]),
               [(sT, mk_s2)], (sV, 1))
            mk_s2e = sV.n
            for (a, b) in chunks(0, T, 512):
                MM(psBig[:, a:b], c1024[:], row_s1[0:1, a:b], True, True,
                   waits=[(sV, mk_s2e)] if a == 0 else (),
                   inc=(sT, 1) if b == T else None)
            mk_bc1 = sT.n
            em("v", lambda e: e.tensor_copy(mu_b[:], psBig[:, 0:T]), [(sT, mk_bc1)], (sV, 1))
            mk_mu = sV.n
            for (a, b) in chunks(0, T, 512):
                MM(psBig[:, a:b], c1024[:], row_s2[0:1, a:b], True, True,
                   waits=[(sV, mk_mu)] if a == 0 else (),
                   inc=(sT, 1) if b == T else None)
            mk_bc2 = sT.n
            em("v", lambda e: e.tensor_mul(af[:, 0:T], mu_b[:], mu_b[:]))
            em("v", lambda e: e.tensor_sub(af[:, 0:T], psBig[:, 0:T], af[:, 0:T]),
               [(sT, mk_bc2)], (sV, 1))
            mk_var = sV.n
            em("s", lambda e: e.activation(af[:, 0:T], af[:, 0:T], AFT.Ln,
                                           bias=eps_col[:]),
               [(sV, mk_var), (sM, mk_memset)], (sS, 1))
            em("s", lambda e: e.activation(inv_b[:], af[:, 0:T], AFT.Exp, scale=-0.5),
               (), (sS, 1))
            mk_inv = sS.n
            set_free("BigL", (sV, mk_var))
            set_free("BigH", (sV, mk_var))
            return [(sV, mk_mu), (sS, mk_inv)]

        def ln_apply(src_of, dst_of, scale_of, bias_of, mk_stats, extra_w=()):
            marks = []
            prev_s = 0
            for dc in range(ND):
                w = (list(mk_stats) + list(extra_w)) if dc == 0 else [(sS, prev_s)]
                em("v", lambda e, x_=src_of(dc): e.tensor_sub(tmp1[:], x_, mu_b[:]), w)
                em("v", lambda e: e.tensor_mul(tmp1[:], tmp1[:], inv_b[:]), (), (sV, 1))
                mkt = sV.n
                if scale_of is None:
                    em("s", lambda e, d_=dst_of(dc): e.copy(d_, tmp1[:]), [(sV, mkt)], (sS, 1))
                else:
                    em("s", lambda e, d_=dst_of(dc), sc=scale_of(dc), bi=bias_of(dc):
                       e.activation(d_, tmp1[:], AFT.Identity, bias=bi, scale=sc),
                       [(sV, mkt)], (sS, 1))
                prev_s = sS.n
                marks.append(prev_s)
            return marks

        mk1 = ln_stats(xc, [], w_dc=lambda dc: [(sX, mk_x[dc])])
        mk_h1 = ln_apply(xc, hc,
                         lambda dc: mod_sb[:, 8 + dc:9 + dc],
                         lambda dc: mod_sb[:, dc:dc + 1], mk1,
                         extra_w=[(sV, mk_mod)])
        mk2 = ln_stats(hc, [], w_dc=lambda dc: [(sS, mk_h1[dc])])
        mk_xn = ln_apply(hc, hc,
                         lambda dc: gam_sb[:, dc:dc + 1],
                         lambda dc: bet_sb[:, dc:dc + 1], mk2)

        gate(2)
        tap("d_xn", hx[:], [(sS, mk_xn[-1])])
        for kc in range(NQT):
            em("g", lambda e, kc=kc: e.memset(
                vaug(kc).rearrange("p (a b) -> p a b", b=192)[:, :, 64:128], 1.0),
                [(sT, mk_modmm)] if kc == 0 else (), (sM, 1))
        mk_vones = sM.n
        if upto == 30:
            B.stopped = True

        mk_vtok_all = 0
        for tt in range(NQT):
            rn = "BigL" if tt % 2 == 0 else "BigH"
            reg = psBig[:, 0:1024] if tt % 2 == 0 else psBig[:, 1024:2048]
            w0 = [(sWV, mk_wv[-1]), (sS, mk_xn[-1])] + free_of(rn)
            for dc in range(ND):
                for (a, b) in chunks(0, D, 512):
                    MM(reg[:, a:b],
                       hx[:, dc * T + tt * 128: dc * T + (tt + 1) * 128],
                       arena[:, o_wv + dc * 1024 + a: o_wv + dc * 1024 + b],
                       start=(dc == 0), stop=(dc == ND - 1),
                       waits=w0 if (dc == 0 and a == 0) else (),
                       inc=(sT, 1) if (dc == ND - 1 and b == D) else None)
            mkv = sT.n
            em("v", lambda e, tt=tt, reg=reg: e.tensor_copy(
                vaug(tt).rearrange("p (a b) -> p a b", b=192)[:, :, 0:64],
                reg.rearrange("p (a b) -> p a b", b=128)[:, :, 0:64]),
               [(sT, mkv), (sM, mk_vones)], (sV, 1))
            em("v", lambda e, tt=tt, reg=reg: e.tensor_copy(
                vaug(tt).rearrange("p (a b) -> p a b", b=192)[:, :, 128:192],
                reg.rearrange("p (a b) -> p a b", b=128)[:, :, 64:128]), (), (sV, 1))
            set_free(rn, (sV, sV.n))
            mk_vtok_all = sV.n
            if upto == 31 and tt >= 1:
                B.stopped = True
            if upto == 32 and tt >= 2:
                B.stopped = True

        gate(3)
        mk_vtok_mm = sT.n
        for j, kb in enumerate((KaA, KaB, KaC)):
            em("g", lambda e, kb=kb: e.memset(kb[64:65, :], 1.0),
               [(sT, mk_vtok_mm)] if j == 0 else (), (sM, 1))
        mk_kones = sM.n

        PA = [psBig[:, 1024:1536], psBig[:, 1536:2048]]
        PAn = ["PA0", "PA1"]
        for nm in PAn:
            set_free(nm, *FREE.get("BigH", []))
        set_free("CTX", *FREE.get("BigL", []))

        wqk_mk = {}
        wqk_free = {}
        for i in range(2):
            wqk_mk[i] = gdma(sQK[i % 2], wqk_buf(i), wqkT.ap()[i // 2, i % 2],
                             waits=[(sT, sT.n)] if i == 0 else ())
        mk_qm = {}
        mk_mrow_h = {}
        FREEPT = {0: (sT, 0), 1: (sT, 0)}
        AFFREE = {0: (sV, 0), 1: (sV, 0)}
        AVG = [0]
        pa_ci = [0]
        evi = [0]
        STOPMK = {}

        def gen_qkv(p):
            for qk in range(2):
                i = p * 2 + qk
                of = qm_of if qk == 0 else ka_of
                dstM = [of(2 * p), of(2 * p + 1)]
                mkq = 0
                for q4 in range(T // 512):
                    ci = pa_ci[0]
                    pa_ci[0] += 1
                    rn = PAn[ci % 2]
                    reg = PA[ci % 2]
                    w0 = [(sQK[i % 2], wqk_mk[i]), (sS, mk_xn[-1])] + free_of(rn)
                    for dc in range(ND):
                        MM(reg[:, 0:512],
                           wqk_buf(i)[:, dc * 128:(dc + 1) * 128],
                           hx[:, dc * T + q4 * 512:dc * T + (q4 + 1) * 512],
                           start=(dc == 0), stop=(dc == ND - 1),
                           waits=w0 if dc == 0 else (),
                           inc=(sT, 1) if dc == ND - 1 else None)
                    mkq = sT.n
                    sl = slice(q4 * 512, (q4 + 1) * 512)
                    if ci % 2 == 0:
                        em("s", lambda e, d=dstM[0], sl=sl, reg=reg:
                           e.copy(d[0:64, sl], reg[0:64, 0:512]),
                           [(sT, mkq), (sM, mk_kones)], (sS, 1))
                        em("s", lambda e, d=dstM[1], sl=sl, reg=reg:
                           e.copy(d[0:64, sl], reg[64:128, 0:512]), (), (sS, 1))
                        set_free(rn, (sS, sS.n))
                    else:
                        em("v", lambda e, d=dstM[0], sl=sl, reg=reg:
                           e.tensor_copy(d[0:64, sl], reg[0:64, 0:512]),
                           [(sT, mkq), (sM, mk_kones)], (sV, 1))
                        em("v", lambda e, d=dstM[1], sl=sl, reg=reg:
                           e.tensor_copy(d[0:64, sl], reg[64:128, 0:512]), (), (sV, 1))
                        set_free(rn, (sV, sV.n))
                    yield
                wqk_free[i] = (sT, mkq)
                if i + 2 < 2 * NP:
                    i2 = i + 2
                    wqk_mk[i2] = gdma(sQK[i2 % 2], wqk_buf(i2),
                                      wqkT.ap()[i2 // 2, i2 % 2], waits=[wqk_free[i]])
            mk_qm[2 * p] = [(sS, sS.n), (sV, sV.n)]
            mk_qm[2 * p + 1] = mk_qm[2 * p]

        def gen_passA(h):
            p, hl = divmod(h, 2)
            first = [True]
            for qt in range(NQT):
                klen = min(T, ((qt * 128) // 512) * 512 + 512)
                c0A = (qt * 128 // 512) * 512
                dlt = qt * 128 - c0A
                for j, (a, b) in enumerate(chunks(0, klen, 512)):
                    ci = pa_ci[0]
                    pa_ci[0] += 1
                    rn = PAn[ci % 2]
                    reg = PA[ci % 2]
                    w = list(free_of(rn))
                    if first[0]:
                        w += mk_qm[h] + [(sC, mk_consts_g)]
                        first[0] = False
                    diagc = (a == c0A)
                    MM(reg[:, 0:b - a],
                       qm_of(h)[0:64, qt * 128:(qt + 1) * 128],
                       ka_of(h)[0:64, a:b], True, not diagc, waits=w,
                       inc=None if diagc else (sT, 1))
                    if diagc:
                        mo = 384 - dlt
                        MM(reg[:, dlt:b - a], ident[:], maskC[:, mo + dlt:mo + (b - a)],
                           False, True, inc=(sT, 1))
                    mkmm = sT.n
                    col = hl * NQT + qt
                    tcol = 2 * NQT + hl
                    tgt = col if j == 0 else tcol
                    em("v", lambda e, reg=reg, tgt=tgt, w_=b - a:
                       e.reduce_max(m_all[:, tgt:tgt + 1], reg[:, 0:w_], axis=AX,
                                    negate=True),
                       [(sT, mkmm)], (sV, 1) if j == 0 else None)
                    if j > 0:
                        em("v", lambda e, col=col, tcol=tcol:
                           e.tensor_tensor(m_all[:, col:col + 1], m_all[:, col:col + 1],
                                           m_all[:, tcol:tcol + 1], ALU.min),
                           (), (sV, 1))
                    set_free(rn, (sV, sV.n))
                    yield
            for b4 in range(NQT // 4):
                ci = pa_ci[0]
                pa_ci[0] += 1
                rn = PAn[ci % 2]
                reg = PA[ci % 2]
                for k4 in range(4):
                    qt = b4 * 4 + k4
                    col = hl * NQT + qt
                    MM(reg[0:1, k4 * 128:(k4 + 1) * 128],
                       m_all[:, col:col + 1], ident[:], True, True,
                       waits=(free_of(rn) + [(sV, sV.n)]) if k4 == 0 else (),
                       inc=(sT, 1) if k4 == 3 else None)
                mkr = sT.n
                em("s", lambda e, h=h, b4=b4, reg=reg:
                   e.copy(qm_of(h)[64:65, b4 * 512:(b4 + 1) * 512], reg[0:1, 0:512]),
                   [(sT, mkr)], (sS, 1))
                set_free(rn, (sS, sS.n))
                yield
            mk_mrow_h[h] = sS.n

        def emit_av(h, half, pend):
            (kc, q0, c0, bcells, bi, mkexp) = pend
            H0 = half * 1024
            off = 192 * (h // 2) + (64 if h % 2 else 0)
            lo = bcells[0][0]
            ptb = pt[bi % 2]
            avw = [(sS, mkexp)]
            if kc == 0:
                avw += [(sV, AVG[0])] + free_of("CTX") + dbgj()
            for (u0, u1) in bcells:
                stop = (kc == min(NQT - 1, (u0 + 384) // 128))
                MM(psBig[:, u0 - H0:u1 - H0],
                   vaug(kc)[:, off:off + 128],
                   ptb[:, u0 - lo:u1 - lo],
                   start=(kc == 0), stop=stop,
                   waits=avw if u0 == lo else (),
                   inc=(sT, 1) if (stop or u1 == bcells[-1][1]) else None)
                if stop:
                    STOPMK[u0] = sT.n
            FREEPT[bi % 2] = (sT, sT.n)

        from collections import deque
        bgq = deque()

        def bg_step(n=1):
            while n > 0 and bgq:
                try:
                    next(bgq[0])
                    n -= 1
                except StopIteration:
                    bgq.popleft()

        def bg_drain():
            while bgq:
                try:
                    next(bgq[0])
                except StopIteration:
                    bgq.popleft()

        mk_ctx_last = 0
        for _ in gen_qkv(0):
            pass
        for _ in gen_passA(0):
            pass

        for h in range(2 * NP):
            p, hl = divmod(h, 2)
            bg_drain()
            assert h in mk_mrow_h
            if hl == 0:
                bgq.append(gen_passA(h + 1))
            elif p + 1 < NP:
                bgq.append(gen_qkv(p + 1))
                bgq.append(gen_passA(h + 1))
            bi = 0
            for half, (H0, H1) in enumerate(chunks(0, T, 1024)):
                HW = H1 - H0
                batches = []
                for kc in range(H1 // 128):
                    q0 = kc * 128
                    c0 = (q0 // 512) * 512
                    lo0 = max(c0, H0)
                    cells = chunks(lo0, H1, 512)
                    for bs in range(0, len(cells), 2):
                        batches.append((kc, q0, c0, cells[bs:bs + 2],
                                        bs == 0 and c0 >= H0))
                AVG[0] = mk_ctx_last
                pend = None
                for (kc, q0, c0, bcells, diag) in batches:
                    lo = bcells[0][0]
                    hi_ = bcells[-1][1]
                    wdt = hi_ - lo
                    rn2 = "B2f" if bi % 2 == 0 else "C2f"
                    reg = psB2 if bi % 2 == 0 else psC2
                    ptb = pt[bi % 2]
                    w = list(free_of(rn2))
                    if kc == 0 and half == 0:
                        w += [(sS, mk_mrow_h[h])]
                    ncell = len(bcells)
                    for idx, (u0, u1) in enumerate(bcells):
                        isd = diag and idx == 0
                        MM(reg[:, u0 - lo:u1 - lo],
                           ka_of(h)[:, kc * 128:(kc + 1) * 128],
                           qm_of(h)[:, u0:u1], True, not isd,
                           waits=w if u0 == lo else (),
                           inc=(sT, 1) if (u1 == hi_ and not isd) else None)
                        if isd:
                            dlt2 = q0 - c0
                            mo = 896 + 384 - dlt2
                            w2 = min(u1 - lo, dlt2 + 128)
                            MM(reg[:, 0:w2], ident[:],
                               maskC[:, mo:mo + w2], False, True,
                               inc=(sT, 1) if ncell == 1 else None)
                    mksc = sT.n
                    wexp = [(sT, mksc), FREEPT[bi % 2]]
                    em("s", lambda e, ptb=ptb, reg=reg, wdt=wdt:
                       e.activation(ptb[:, 0:wdt], reg[:, 0:wdt], AFT.Exp, scale=ISQ),
                       wexp, (sS, 1))
                    mkexp = sS.n
                    set_free(rn2, (sS, mkexp))
                    if pend is not None:
                        emit_av(h, half, pend)
                        bg_step(2)
                    pend = (kc, q0, c0, bcells, bi, mkexp)
                    bi += 1
                if pend is not None:
                    emit_av(h, half, pend)
                mk_av = sT.n
                crows = psBig[0:64] if hl == 0 else psBig[64:128]
                drows = psBig[64:128] if hl == 0 else psBig[0:64]
                cs = ctx(p)[hl * 64:hl * 64 + 64, :]
                for (u0, u1) in chunks(0, HW, 512):
                    sl = evi[0] % 2
                    aslot = af[0:64, sl * 1024:sl * 1024 + 512]
                    tslot = af[0:64, sl * 1024 + 512:sl * 1024 + 1024]
                    em("s", lambda e, drows=drows, u0=u0, u1=u1, tslot=tslot:
                       e.activation(tslot, drows[:, u0:u1], AFT.Ln),
                       [(sT, STOPMK.get(H0 + u0, mk_av))] +
                       (dbgj() if u0 == 0 else []) + [AFFREE[sl]],
                       (sS, 1))
                    em("s", lambda e, aslot=aslot, tslot=tslot:
                       e.activation(aslot, tslot, AFT.Exp, scale=-1.0), (), (sS, 1))
                    mkrc = sS.n
                    em("v", lambda e, cs=cs, crows=crows, H0=H0, u0=u0, u1=u1,
                       aslot=aslot:
                       e.tensor_mul(cs[:, H0 + u0:H0 + u1], crows[:, u0:u1], aslot),
                       [(sS, mkrc)], (sV, 1))
                    AFFREE[sl] = (sV, sV.n)
                    evi[0] += 1
                mk_ctx_last = sV.n
                set_free("CTX", (sV, mk_ctx_last))
                bg_step(3)
        bg_drain()
        gate(5)
        tap("d_ctx", arena[:, o_ctx:o_ctx + ND * T], [(sV, sV.n), (sS, sS.n)])
        mk_attn_done = (sT.n, sV.n, sS.n)

        wo_mk = {}
        wo_free = {}
        for ec in range(min(2, ND)):
            wo_mk[ec] = gdma(sWO[ec % 2], wo_buf(ec), woT.ap()[ec],
                             waits=[(sT, mk_attn_done[0])])
        regW_name = ["BigL", "BigH", "C2w"]
        regW_ap = [psBig[:, 0:1024], psBig[:, 1024:2048], psC2[:, 0:1024]]
        set_free("BigL", *FREE.get("CTX", []))
        set_free("BigH", *(FREE.get("PA0", []) + FREE.get("PA1", [])))
        set_free("C2w", *FREE.get("C2f", []))
        ri = 0
        mk_x2_all = 0
        mk_x2 = {}
        for ec in range(ND):
            for th, (a, b) in enumerate(chunks(0, T, 1024)):
                rn = regW_name[ri % 3]
                reg = regW_ap[ri % 3]
                w0 = [(sWO[ec % 2], wo_mk[ec])] + free_of(rn) + \
                    ([(sV, mk_attn_done[1]), (sS, mk_attn_done[2])] if ri == 0 else [])
                for dc in range(ND):
                    for (u0, u1) in chunks(a, b, 512):
                        MM(reg[:, u0 - a:u1 - a],
                           wo_buf(ec)[:, dc * 128:(dc + 1) * 128],
                           ctx(dc)[:, u0:u1],
                           start=(dc == 0), stop=(dc == ND - 1),
                           waits=w0 if (dc == 0 and u0 == a) else (),
                           inc=(sT, 1) if (dc == ND - 1 and u1 == b) else None)
                mkw = sT.n
                wo_free[ec] = (sT, mkw)
                em("v", lambda e, ec=ec, a=a, b=b, reg=reg:
                   e.scalar_tensor_tensor(x_sb[:, ec * T + a:ec * T + b], reg[:, 0:b - a],
                                          mod_sb[:, 16 + ec:17 + ec],
                                          x_sb[:, ec * T + a:ec * T + b],
                                          ALU.mult, ALU.add),
                   [(sT, mkw)], (sV, 1))
                set_free(rn, (sV, sV.n))
                ri += 1
            if ec + 2 < ND:
                wo_mk[ec + 2] = gdma(sWO[ec % 2], wo_buf(ec + 2), woT.ap()[ec + 2],
                                     waits=[wo_free[ec]])
            mk_x2[ec] = sV.n
            mk_x2_all = sV.n

        gate(6)
        tap("d_x2", x_sb[:], [(sV, mk_x2_all)])
        mk3 = ln_stats(xc, [], w_dc=lambda dc: [(sV, mk_x2[dc])])
        mk_h2 = ln_apply(xc, hc,
                         lambda dc: mod_sb[:, 32 + dc:33 + dc],
                         lambda dc: mod_sb[:, 24 + dc:25 + dc], mk3)
        for dc in range(ND):
            for (a, b) in chunks(0, T, 512):
                MM(psBig[0:8, a:b], wg_sb[:, dc * 8:(dc + 1) * 8],
                   hc(dc)[:, a:b], start=(dc == 0), stop=(dc == ND - 1),
                   waits=([(sS, mk_h2[-1]), (sC, mk_consts_g)] + free_of("BigL")
                          + free_of("BigH")) if (dc == 0 and a == 0) else (),
                   inc=(sT, 1) if (dc == ND - 1 and b == T) else None)
        mk_lgm = sT.n
        em("v", lambda e: e.tensor_copy(lg_sb[:], psBig[0:8, 0:T]), [(sT, mk_lgm)], (sV, 1))
        mk_lge = sV.n
        tap("d_lg", lg_sb[:, :], [(sV, mk_lge)])
        for tt in range(NQT):
            MM(psC2[:, tt * 8:(tt + 1) * 8], lg_sb[0:8, tt * 128:(tt + 1) * 128],
               ident[0:8, 0:8], True, True,
               waits=([(sV, mk_lge)] + free_of("C2w") + free_of("C2f"))
               if tt == 0 else (),
               inc=(sT, 1) if tt == NQT - 1 else None)
        mk_gt = sT.n
        em("v", lambda e: e.tensor_copy(gtok_sb[:], psC2[:, 0:NQT * 8]), [(sT, mk_gt)], (sV, 1))
        mk_gte = sV.n
        set_free("C2f", (sV, mk_gte))
        set_free("C2w", (sV, mk_gte))

        NEQ = NQT * NE
        rr = lambda t_: t_.rearrange("p (a b) -> p a b", b=NE)
        r1 = lambda t_: t_.rearrange("p (a b) -> p a b", b=1)
        m1 = g3_sb[:, 0:NQT]
        tr = g3_sb[:, NQT:2 * NQT]
        m2 = g3_sb[:, 2 * NQT:3 * NQT]
        GT = gtok_sb[:, 0:NEQ]
        G1 = g1_sb[:, 0:NEQ]
        G2 = g2_sb[:, 0:NEQ]
        IX = idxm_sb[:, 0:NEQ]
        em("v", lambda e: e.reduce_max(m1, rr(GT), axis=AX), [(sV, mk_gte)])
        for j in range(NE):
            em("v", lambda e, j=j: e.tensor_tensor(rr(G2)[:, :, j:j + 1],
                                                   rr(GT)[:, :, j:j + 1], r1(m1), ALU.is_equal))
        em("v", lambda e: e.tensor_mul(G2, G2, IX))
        em("v", lambda e: e.tensor_scalar_add(G2, G2, 99.0))
        em("v", lambda e: e.tensor_reduce(tr, rr(G2), axis=AX, op=ALU.min))
        em("v", lambda e: e.tensor_scalar_add(tr, tr, -99.0))
        for j in range(NE):
            em("v", lambda e, j=j: e.tensor_tensor(rr(G2)[:, :, j:j + 1],
                                                   rr(IX)[:, :, j:j + 1], r1(tr), ALU.is_equal))
        em("v", lambda e: e.scalar_tensor_tensor(G1, G2, -1e38, GT, ALU.mult, ALU.add))
        em("v", lambda e: e.reduce_max(m2, rr(G1), axis=AX))
        for j in range(NE):
            em("v", lambda e, j=j: e.tensor_tensor(rr(G1)[:, :, j:j + 1],
                                                   rr(G1)[:, :, j:j + 1], r1(m2), ALU.is_equal))
        em("v", lambda e: e.tensor_mul(G1, G1, IX))
        em("v", lambda e: e.tensor_scalar_add(G1, G1, 99.0))
        em("v", lambda e: e.tensor_reduce(tr, rr(G1), axis=AX, op=ALU.min))
        em("v", lambda e: e.tensor_scalar_add(tr, tr, -99.0))
        for j in range(NE):
            em("v", lambda e, j=j: e.tensor_tensor(rr(G1)[:, :, j:j + 1],
                                                   rr(IX)[:, :, j:j + 1], r1(tr), ALU.is_equal))
        em("v", lambda e: e.tensor_sub(tr, m2, m1), (), (sV, 1))
        mk_gd = sV.n
        em("s", lambda e: e.activation(tr, tr, AFT.Exp), [(sV, mk_gd)], (sS, 1))
        mk_ge = sS.n
        em("v", lambda e: e.tensor_scalar_add(tr, tr, 1.0), [(sS, mk_ge)])
        em("v", lambda e: e.reciprocal(tr, tr))
        em("v", lambda e: e.tensor_scalar(m2, tr, -1.0, 1.0, ALU.mult, ALU.add))
        for j in range(NE):
            em("v", lambda e, j=j: e.tensor_tensor(rr(G2)[:, :, j:j + 1],
                                                   rr(G2)[:, :, j:j + 1], r1(tr), ALU.mult))
        for j in range(NE):
            em("v", lambda e, j=j: e.tensor_tensor(rr(G1)[:, :, j:j + 1],
                                                   rr(G1)[:, :, j:j + 1], r1(m2), ALU.mult))
        em("v", lambda e: e.tensor_add(G1, G1, G2))
        em("v", lambda e: e.tensor_mul(G1, G1, alpha_sb[:, 0:NEQ]), [(sC, mk_consts_g)])
        em("v", lambda e: e.tensor_copy(G1, G1), (), (sV, 1))
        mk_comb = sV.n
        mk_cfm = mk_comb
        tap("d_comb", g1_sb[:, 0:NEQ], [(sV, mk_comb)])

        gate(7)
        mk4 = ln_stats(hc, [], w_dc=lambda dc: [(sS, mk_h2[dc])])
        mk_xn2 = ln_apply(hc, hc, None, None, mk4, extra_w=[(sV, mk_cfm)])
        tap("d_xn2", hx[:], [(sS, mk_xn2[-1])])

        NT2 = TH // 512 if TH >= 512 else 1
        t2ch = chunks(0, TH, 512)
        gi1 = 0
        gi2 = 0
        gate(8)
        fc1_free = {}
        fc2_free = {}
        mk_hdn_free = sT.n
        mk_cbev_last = 0
        mk_combb_free = {0: (sV, 0), 1: (sV, 0)}
        mk_out_free = {0: (sO[0], 0), 1: (sO[1], 0)}
        y_last = 0
        regY_name = ["B2y", "C2y"]
        set_free("B2y", *(FREE.get("B2f", []) or [(sV, 0)]))
        set_free("C2y", (sV, mk_gte))
        set_free("F1a", *(FREE.get("B2f", []) or [(sV, 0)]))
        set_free("F1b", (sV, mk_gte))

        for pi in range(2):
            t0 = pi * TH
            for ei in range(NE):
                tt0, tt1 = t0 // 128, (t0 + TH) // 128
                for tt in range(tt0, tt1):
                    MM(psBig[0:1, (tt - tt0) * 128:(tt - tt0) * 128 + 128],
                       g1_sb[:, tt * NE + ei:tt * NE + ei + 1], ident[:], True, True,
                       waits=(free_of("BigL") + [(sV, mk_comb)]) if tt == tt0 else (),
                       inc=(sT, 1) if tt == tt1 - 1 else None)
                mkct = sT.n
                em("v", lambda e: e.tensor_copy(rows_bf[0:1, 0:TH], psBig[0:1, 0:TH]),
                   [(sT, mkct)], (sV, 1))
                mk_crow = sV.n
                for (a, b) in t2ch:
                    MM(psBig[:, a:b], onesrow[:], rows_bf[0:1, a:b],
                       True, True,
                       waits=([(sV, mk_crow), mk_combb_free[0]]) if a == 0 else (),
                       inc=(sT, 1) if b == TH else None)
                mkcb = sT.n
                em("v", lambda e, ei=ei: e.tensor_copy(combb(ei), psBig[:, 0:TH]),
                   [(sT, mkcb)], (sV, 1))
                mk_cbev = sV.n
                set_free("BigL", (sV, mk_cbev))

                for dc in range(ND):
                    em("s", lambda e, ei=ei, dc=dc, t0=t0:
                       e.activation(hdn_ap(dc), hc(dc)[:, t0:t0 + TH], AFT.Identity,
                                    bias=lnb_sb[:, ei * ND + dc:ei * ND + dc + 1],
                                    scale=lnw_sb[:, ei * ND + dc:ei * ND + dc + 1]),
                       (([(sS, mk_xn2[-1]), (sT, mk_hdn_free), (sSM, mk_smalls)] + dbgj())
                        if dc == 0 else ()), (sS, 1))
                mk_hdn = sS.n
                if pi == 0 and ei == 0:
                    tap("d_hdn", arena[:, o_hdn:o_hdn + ND * TH], [(sS, mk_hdn)])

                for fc in range(NFC):
                    if fc % 2 == 0:
                        w = [fc1_free[gi1 - 2]] if gi1 >= 2 else [(sT, mk_attn_done[0]),
                                                                  (sV, mk_cfm)]
                        mkd = gdma(sF1[gi1 % 2], fc1buf2(gi1),
                                   fc1T.ap()[ei, fc // 2], waits=w)
                    rn = "F1a" if fc % 2 == 0 else "F1b"
                    reg = psB2 if fc % 2 == 0 else psC2
                    w0 = [(sF1[gi1 % 2], mkd), (sS, mk_hdn)] + free_of(rn)
                    for dc in range(ND):
                        for (a, b) in t2ch:
                            MM(reg[:, a:b],
                               fc1buf2(gi1)[:, (fc % 2) * 1024 + dc * 128:(fc % 2) * 1024 + (dc + 1) * 128],
                               hdn_ap(dc)[:, a:b],
                               start=(dc == 0), stop=(dc == ND - 1),
                               waits=w0 if (dc == 0 and a == 0) else (),
                               inc=(sT, 1) if (dc == ND - 1 and b == TH) else None)
                    mkf1 = sT.n
                    if fc % 2 == 1:
                        fc1_free[gi1] = (sT, mkf1)
                        gi1 += 1
                    em("s", lambda e, ei=ei, fc=fc, reg=reg:
                       e.activation(A_ap(fc), reg[:, 0:TH], AFT.Gelu_apprx_tanh,
                                    bias=fc1b_sb[:, ei * NFC + fc:ei * NFC + fc + 1]),
                       [(sT, mkf1), (sC, mk_consts_g)] + (dbgj() if fc == 0 else []), (sS, 1))
                    set_free(rn, (sS, sS.n))
                mk_gelu = sS.n
                if pi == 0 and ei == 0:
                    tap("d_A", A_ap(0), [(sS, mk_gelu)])
                mk_hdn_free = sT.n

                for hcx in range(ND):
                    rn = regY_name[hcx % 2]
                    regY = psB2 if hcx % 2 == 0 else psC2
                    mkd = {}
                    for hf in range(2):
                        gih = gi2 * 2 + hf
                        w = [fc2_free[gih - 2]] if gih >= 2 else [(sT, mk_attn_done[0]),
                                                                  (sV, mk_cfm)]
                        mkd[hf] = (sF2[gih % 2],
                                   gdma(sF2[gih % 2], fc2buf(gih),
                                        fc2T.ap()[ei, hcx][:, hf * 2048:(hf + 1) * 2048],
                                        waits=w))
                    w0 = [(sS, mk_gelu)] + free_of(rn) + \
                        (free_of("F1a") + free_of("F1b") if hcx <= 1 else [])
                    for fs in range(NFC):
                        hf = fs // 16
                        gih = gi2 * 2 + hf
                        wf = ([mkd[hf]] + (w0 if fs == 0 else []))
                        for (a, b) in t2ch:
                            MM(regY[:, a:b],
                               fc2buf(gih)[:, (fs % 16) * 128:(fs % 16) * 128 + 128],
                               A_ap(fs)[:, a:b],
                               start=(fs == 0), stop=(fs == NFC - 1),
                               waits=wf if (a == 0 and fs % 16 == 0) else (),
                               inc=(sT, 1) if (b == TH and fs % 16 == 15) else None)
                        if fs % 16 == 15:
                            fc2_free[gi2 * 2 + hf] = (sT, sT.n)
                    mkf2 = sT.n
                    bcol = fc2bc_sb[:, ei * ND + hcx:ei * ND + hcx + 1]
                    for (a, b) in t2ch:
                        if ei == 0:
                            em("v", lambda e, regY=regY, a=a, b=b, ei=ei, bcol=bcol, hcx=hcx:
                               e.scalar_tensor_tensor(acc_ap_l(hcx)[:, a:b], regY[:, a:b],
                                                      bcol, combb(ei)[:, a:b],
                                                      ALU.add, ALU.mult),
                               [(sT, mkf2), (sV, mk_cbev), (sSM, mk_smalls)], (sV, 1))
                        else:
                            em("v", lambda e, regY=regY, a=a, b=b, ei=ei, bcol=bcol:
                               e.scalar_tensor_tensor(af[:, 1024:1024 + b - a], regY[:, a:b],
                                                      bcol, combb(ei)[:, a:b],
                                                      ALU.add, ALU.mult),
                               [(sT, mkf2), (sV, mk_cbev)], None)
                            em("v", lambda e, hcx=hcx, a=a, b=b: e.tensor_add(
                                acc_ap_l(hcx)[:, a:b], acc_ap_l(hcx)[:, a:b],
                                af[:, 1024:1024 + b - a]),
                               (), (sV, 1))
                    set_free(rn, (sV, sV.n))
                    gi2 += 1
                mk_acc_w = sV.n
                mk_combb_free[0] = (sV, mk_acc_w)
                set_free("F1a", (sV, mk_acc_w))
                set_free("F1b", (sV, mk_acc_w))

            if pi == 0:
                tap("d_acc", arena[:, o_acc:o_acc + ND * TH], [(sV, mk_acc_w)])
            if upto <= 9 and pi == 0:
                B.stopped = True
            for hcx in range(ND):
                for (a, b) in t2ch:
                    oi = hcx * max(1, NT2) + a // 512
                    em("v", lambda e, hcx=hcx, a=a, b=b, oi=oi, t0=t0: e.scalar_tensor_tensor(
                        out_st(oi)[:, 0:b - a], acc_ap(hcx)[:, a:b],
                        mod_sb[:, 40 + hcx:41 + hcx],
                        x_sb[:, hcx * T + t0 + a:hcx * T + t0 + b],
                        ALU.mult, ALU.add),
                       [(sV, mk_acc_w), mk_out_free[oi % 2]], (sV, 1))
                    mko = sV.n
                    y_last = ydma(sO[oi % 2], outT.ap()[hcx][:, t0 + a:t0 + b],
                                  out_st(oi)[:, 0:b - a], waits=[(sV, mko)])
                    mk_out_free[oi % 2] = (sO[oi % 2], y_last)

        B.raw("y", lambda e: (e.wait_ge(sO[0].h, sO[0].n), e.wait_ge(sO[1].h, sO[1].n),
                              (e.wait_ge(sDB.h, sDB.n) if dbg else None)), ())

        B.check_deadlock()

        es.enter_context(nc.allow_low_precision(
            reason="bf16 activations with fp32 psum accumulation; fp32 stats math"))
        with nc.Block() as blk:
            @blk.tensor
            def _(e):
                for f in B.streams["t"]:
                    f(e)

            @blk.vector
            def _(e):
                for f in B.streams["v"]:
                    f(e)

            @blk.scalar
            def _(e):
                for f in B.streams["s"]:
                    f(e)

            @blk.gpsimd
            def _(e):
                for f in B.streams["g"]:
                    f(e)

            @blk.sync
            def _(e):
                for f in B.streams["y"]:
                    f(e)

    return nc



def prep_inputs(b, T, L0_attn_x, L0_attn_ln_gamma, L0_attn_ln_beta, L0_attn_W_qkv,
                L0_attn_W_o, L0_ffn_Wg, L0_ffn_alpha, L0_ffn_ln_w, L0_ffn_ln_b,
                L0_ffn_fc1_w, L0_ffn_fc1_b, L0_ffn_fc2_w, L0_ffn_fc2_b, L0_c, L0_wmod):
    NQT = T // 128
    f = np.float32
    d = {}
    d["xT"] = np.ascontiguousarray(np.asarray(L0_attn_x[b], f)[:T].T.reshape(ND, 128, T))
    d["c_col"] = np.ascontiguousarray(np.asarray(L0_c[b], f).reshape(ND, 128).T)
    wm = np.asarray(L0_wmod, f).reshape(ND, 128, 48, 128)
    d["wmodT"] = np.ascontiguousarray(wm.transpose(1, 2, 0, 3).reshape(128, 48 * ND * 128))
    d["gamma_c"] = np.ascontiguousarray(np.asarray(L0_attn_ln_gamma, f).reshape(ND, 128).T)
    d["beta_c"] = np.ascontiguousarray(np.asarray(L0_attn_ln_beta, f).reshape(ND, 128).T)
    W3 = np.asarray(L0_attn_W_qkv, f).reshape(NH, 3, DH, D)
    wqk = np.zeros((NP, 2, 128, ND * 128), f)
    for p in range(NP):
        for qk in range(2):
            blkw = np.concatenate([W3[2 * p, qk], W3[2 * p + 1, qk]], axis=0)
            wqk[p, qk] = blkw.T.reshape(ND, 128, 128).transpose(1, 0, 2).reshape(128, ND * 128)
    d["wqkT"] = wqk
    Vmov = W3[:, 2].reshape(D, D).T
    d["wvT"] = np.ascontiguousarray(Vmov.reshape(ND, 128, D))
    woL = np.asarray(L0_attn_W_o, f).T
    d["woT"] = np.ascontiguousarray(
        woL.reshape(ND, 128, ND, 128).transpose(2, 1, 0, 3).reshape(ND, 128, ND * 128))
    d["wgT"] = np.ascontiguousarray(
        np.asarray(L0_ffn_Wg, f).reshape(ND, 128, NE).transpose(1, 0, 2).reshape(128, ND * NE))
    d["alpha_tok"] = np.ascontiguousarray(np.tile(np.asarray(L0_ffn_alpha, f), (128, NQT)))
    d["lnw_c"] = np.ascontiguousarray(
        np.asarray(L0_ffn_ln_w, f).reshape(NE, ND, 128).transpose(2, 0, 1).reshape(128, NE * ND))
    d["lnb_c"] = np.ascontiguousarray(
        np.asarray(L0_ffn_ln_b, f).reshape(NE, ND, 128).transpose(2, 0, 1).reshape(128, NE * ND))
    fc1 = np.asarray(L0_ffn_fc1_w, f)
    d["fc1T"] = np.ascontiguousarray(
        fc1.reshape(NE, ND, 128, NFC // 2, 2, 128).transpose(0, 3, 2, 4, 1, 5)
        .reshape(NE, NFC // 2, 128, 2 * ND * 128))
    d["fc1b_c"] = np.ascontiguousarray(
        np.asarray(L0_ffn_fc1_b, f).reshape(NE, NFC, 128).transpose(2, 0, 1).reshape(128, NE * NFC))
    fc2 = np.asarray(L0_ffn_fc2_w, f)
    d["fc2T"] = np.ascontiguousarray(
        fc2.reshape(NE, NFC, 128, ND, 128).transpose(0, 3, 2, 1, 4).reshape(NE, ND, 128, NFC * 128))
    d["fc2b_cD"] = np.ascontiguousarray(
        np.asarray(L0_ffn_fc2_b, f).reshape(NE, ND, 128).transpose(2, 0, 1).reshape(128, NE * ND))
    d["identF"] = np.eye(128, dtype=f)
    uu = np.arange(896)[None, :]
    pp = np.arange(128)[:, None]
    mA = np.where((uu - 384) <= pp, 0.0, -1e38).astype(f)
    mB = np.where((uu - 384) >= pp, 0.0, -1e38).astype(f)
    d["maskC_F"] = np.stack([mA, mB])
    d["idxmF"] = np.ascontiguousarray(np.tile(np.arange(NE, dtype=f) - 99.0, (128, NQT)))
    return d


def kernel(**inputs):
    T = 2048
    nc = build_nc(T)
    in_maps = [prep_inputs(b, T, **inputs) for b in range(8)]
    res = run_bass_kernel_spmd(nc, in_maps, core_ids=list(range(8)))
    outs = []
    for r in res.results:
        o = np.asarray(r["outT"], dtype=np.float32).reshape(D, T)
        outs.append(o.T)
    return np.stack(outs, axis=0)
